# version 02 (final)
# baseline (speedup 1.0000x reference)
from contextlib import ExitStack
import numpy as np
import ml_dtypes
import concourse.bass as bass
import concourse.mybir as mybir
from concourse.bass_utils import run_bass_kernel_spmd
from concourse.bass_core_rust import IndirectOffsetOnAxis

F32 = mybir.dt.float32
BF16 = mybir.dt.bfloat16
I32 = mybir.dt.int32
AF = mybir.ActivationFunctionType
ALU = mybir.AluOpType
AX = mybir.AxisListType
BF = ml_dtypes.bfloat16

NCORE = 8
T = 16384
TPC = 2048
D = 2048
H = 16
HD = 128
E = 8
CAP = 2048
HM = 2816
SEQ = 8192
NTC = TPC // 128
NDC = D // 128
EPS = 1e-5
SC025 = float(HD ** (-0.25))
VW = 132


class _IW:
    def __init__(self, inst, eng):
        self._i = inst
        self._e = eng

    def drain(self):
        self._e.drain()
        return self

    def then_inc(self, sem, k=1):
        self._i.then_inc(sem, k)
        return self


class _EW:
    def __init__(self, eng):
        object.__setattr__(self, "_e", eng)

    def __getattr__(self, name):
        a = getattr(object.__getattribute__(self, "_e"), name)
        if callable(a):
            def f(*args, **kw):
                r = a(*args, **kw)
                if hasattr(r, "then_inc"):
                    return _IW(r, object.__getattribute__(self, "_e"))
                return r
            return f
        return a


class _BlkW:
    def __init__(self, blk):
        self._b = blk

    def _wrap(self, reg):
        def dec(f):
            return reg(lambda e: f(_EW(e)))
        return dec

    @property
    def vector(self):
        return self._wrap(self._b.vector)

    @property
    def scalar(self):
        return self._wrap(self._b.scalar)

    @property
    def tensor(self):
        return self._wrap(self._b.tensor)

    @property
    def sync(self):
        return self._wrap(self._b.sync)

    @property
    def gpsimd(self):
        return self._wrap(self._b.gpsimd)


class SC:
    def __init__(self, h):
        self.h = h
        self.n = 0

    def inc(self, inst, k=1):
        inst.then_inc(self.h, k)
        self.n += k
        return self.n


def build_pre0():
    nc = bass.Bass("TRN2", target_bir_lowering=False)
    ap = lambda t: t.ap()
    x_in = ap(nc.dram_tensor("x", [TPC, D], F32, kind="ExternalInput"))
    wq_in = ap(nc.dram_tensor("wq", [D, D], F32, kind="ExternalInput"))
    wkv_in = ap(nc.dram_tensor("wkv", [D, 2 * D], F32, kind="ExternalInput"))
    bqrow_in = ap(nc.dram_tensor("bqrow", [1, D], F32, kind="ExternalInput"))
    bkvrow_in = ap(nc.dram_tensor("bkvrow", [1, 2 * D], F32, kind="ExternalInput"))
    qbp_in = ap(nc.dram_tensor("qbp", [128, NDC], F32, kind="ExternalInput"))
    qgp_in = ap(nc.dram_tensor("qgp", [128, NDC], F32, kind="ExternalInput"))
    kbp_in = ap(nc.dram_tensor("kbp", [128, NDC], F32, kind="ExternalInput"))
    kgp_in = ap(nc.dram_tensor("kgp", [128, NDC], F32, kind="ExternalInput"))
    wg_in = ap(nc.dram_tensor("wg", [128, NDC * E], F32, kind="ExternalInput"))
    lng_in = ap(nc.dram_tensor("lng", [128, D], F32, kind="ExternalInput"))
    lnb_in = ap(nc.dram_tensor("lnb", [128, D], F32, kind="ExternalInput"))
    cos_in = ap(nc.dram_tensor("cosT", [TPC, D // 2], F32, kind="ExternalInput"))
    sin_in = ap(nc.dram_tensor("sinT", [TPC, D // 2], F32, kind="ExternalInput"))
    idf_in = ap(nc.dram_tensor("identf", [128, 128], F32, kind="ExternalInput"))

    xn_tm_o = ap(nc.dram_tensor("xn_tm", [TPC, D], BF16, kind="ExternalOutput"))
    xn_fm_o = ap(nc.dram_tensor("xn_fm", [NTC, 128, NDC, 128], BF16, kind="ExternalOutput"))
    qnT_o = ap(nc.dram_tensor("qnT", [NTC, 128, H, 128], BF16, kind="ExternalOutput"))
    knT_o = ap(nc.dram_tensor("knT", [NTC, 128, H, 128], BF16, kind="ExternalOutput"))
    vch_o = ap(nc.dram_tensor("vch", [NTC, 128, H, VW], BF16, kind="ExternalOutput"))
    eidx_o = ap(nc.dram_tensor("eidx", [128, NTC], F32, kind="ExternalOutput"))
    gate_o = ap(nc.dram_tensor("gate", [128, NTC], F32, kind="ExternalOutput"))
    pos_o = ap(nc.dram_tensor("pos", [128, NTC], F32, kind="ExternalOutput"))
    cnt_o = ap(nc.dram_tensor("cnt", [8, 1], F32, kind="ExternalOutput"))

    with ExitStack() as _ctx:
        xt = _ctx.enter_context(nc.sbuf_tensor("xt", [128, D], F32))
        xn32 = _ctx.enter_context(nc.sbuf_tensor("xn32", [128, D], F32))
        xnbf = _ctx.enter_context(nc.sbuf_tensor("xnbf", [128, D], BF16))
        xnnt = _ctx.enter_context(nc.sbuf_tensor("xnnt", [128, D], BF16))
        sv = _ctx.enter_context(nc.sbuf_tensor("sv", [128, 28], F32))
        cst = _ctx.enter_context(nc.sbuf_tensor("cst", [128, 2], F32))
        msave = _ctx.enter_context(nc.sbuf_tensor("msave", [128, NTC], F32))
        xfm32 = _ctx.enter_context(nc.sbuf_tensor("xfm32", [128, 2 * 128], F32))
        xnn_fm = _ctx.enter_context(nc.sbuf_tensor("xnn_fm", [128, NDC * TPC], BF16))
        wqs = _ctx.enter_context(nc.sbuf_tensor("wqs", [128, NDC * D], BF16))
        wgs = _ctx.enter_context(nc.sbuf_tensor("wgs", [128, NDC * E], F32))
        gvec = _ctx.enter_context(nc.sbuf_tensor("gvec", [128, NDC], F32))
        bvec = _ctx.enter_context(nc.sbuf_tensor("bvec", [128, NDC], BF16))
        gvec2 = _ctx.enter_context(nc.sbuf_tensor("gvec2", [128, NDC], F32))
        bvec2 = _ctx.enter_context(nc.sbuf_tensor("bvec2", [128, NDC], BF16))
        brtmp = _ctx.enter_context(nc.sbuf_tensor("brtmp", [1, 512], F32))
        colst = _ctx.enter_context(nc.sbuf_tensor("colst", [128, 3 * NTC], F32))
        brow = _ctx.enter_context(nc.sbuf_tensor("brow", [1, D], BF16))
        lngs = _ctx.enter_context(nc.sbuf_tensor("lngs", [128, D], BF16))
        lnbs = _ctx.enter_context(nc.sbuf_tensor("lnbs", [128, D], BF16))
        ct = _ctx.enter_context(nc.sbuf_tensor("ct", [128, D // 2], F32))
        st = _ctx.enter_context(nc.sbuf_tensor("st", [128, D // 2], F32))
        idf = _ctx.enter_context(nc.sbuf_tensor("idf", [128, 128], F32))
        idb = _ctx.enter_context(nc.sbuf_tensor("idb", [128, 128], BF16))
        iota8 = _ctx.enter_context(nc.sbuf_tensor("iota8", [128, 8], F32))
        ones_f = _ctx.enter_context(nc.sbuf_tensor("ones_f", [128, 1], F32))
        ones8 = _ctx.enter_context(nc.sbuf_tensor("ones8", [8, 1], F32))
        ones_b1 = _ctx.enter_context(nc.sbuf_tensor("ones_b1", [1, 128], BF16))
        logsb = _ctx.enter_context(nc.sbuf_tensor("logsb", [8, 128], F32))
        ltm = _ctx.enter_context(nc.sbuf_tensor("ltm", [128, 8], F32))
        probs = _ctx.enter_context(nc.sbuf_tensor("probs", [128, 8], F32))
        oneh = _ctx.enter_context(nc.sbuf_tensor("oneh", [128, 8], F32))
        ohT = _ctx.enter_context(nc.sbuf_tensor("ohT", [8, 128], F32))
        scn = _ctx.enter_context(nc.sbuf_tensor("scn", [8, 128], F32))
        sel = _ctx.enter_context(nc.sbuf_tensor("sel", [8, 128], F32))
        run8 = _ctx.enter_context(nc.sbuf_tensor("run8", [8, 1], F32))
        t8 = _ctx.enter_context(nc.sbuf_tensor("t8", [128, 8], F32))
        cnt8 = _ctx.enter_context(nc.sbuf_tensor("cnt8", [8, 1], F32))
        qrow = _ctx.enter_context(nc.sbuf_tensor("qrow", [128, D], F32))
        qrot = _ctx.enter_context(nc.sbuf_tensor("qrot", [128, D], F32))
        hstat = _ctx.enter_context(nc.sbuf_tensor("hstat", [128, H * 4], F32))
        qnb = xnnt
        rs1 = xn32[:, 0:D // 2]
        rs2 = xn32[:, D // 2:D]
        tstg = _ctx.enter_context(nc.sbuf_tensor("tstg", [128, NDC * 128], BF16))
        vst = _ctx.enter_context(nc.sbuf_tensor("vst", [128, H * VW], BF16))
        psA = _ctx.enter_context(nc.psum_tensor("psA", [128, 1024], F32))
        psB = _ctx.enter_context(nc.psum_tensor("psB", [128, 1024], F32))
        psC = _ctx.enter_context(nc.psum_tensor("psC", [128, 2048], BF16))
        pslog = _ctx.enter_context(nc.psum_tensor("pslog", [8, 128], F32))
        pscnt = _ctx.enter_context(nc.psum_tensor("pscnt", [8, 1], F32))
        m_x = _ctx.enter_context(nc.semaphore("m_x"))
        m_rope = _ctx.enter_context(nc.semaphore("m_rope"))
        m_w = _ctx.enter_context(nc.semaphore("m_w"))
        m_sm = _ctx.enter_context(nc.semaphore("m_sm"))
        m_t = _ctx.enter_context(nc.semaphore("m_t"))
        m_v = _ctx.enter_context(nc.semaphore("m_v"))
        m_s = _ctx.enter_context(nc.semaphore("m_s"))
        m_g = _ctx.enter_context(nc.semaphore("m_g"))
        m_o = _ctx.enter_context(nc.semaphore("m_o"))
        m_oxn = _ctx.enter_context(nc.semaphore("m_oxn"))
        m_oxf = _ctx.enter_context(nc.semaphore("m_oxf"))
        m_orow = _ctx.enter_context(nc.semaphore("m_orow"))
        m_oq = _ctx.enter_context(nc.semaphore("m_oq"))
        m_ok = _ctx.enter_context(nc.semaphore("m_ok"))
        m_ov = _ctx.enter_context(nc.semaphore("m_ov"))
        m_br = _ctx.enter_context(nc.semaphore("m_br"))
        blk = _BlkW(_ctx.enter_context(nc.Block()))
        s_x, s_rope, s_w = SC(m_x), SC(m_rope), SC(m_w)
        s_sm, s_t, s_v = SC(m_sm), SC(m_t), SC(m_v)
        s_s, s_g, s_o = SC(m_s), SC(m_g), SC(m_o)
        o_xn, o_xf, o_row = SC(m_oxn), SC(m_oxf), SC(m_orow)
        o_tag = {'q': SC(m_oq), 'k': SC(m_ok), 'v': SC(m_ov)}
        o_sem = {'q': m_oq, 'k': m_ok, 'v': m_ov}
        s_br = SC(m_br)

        @blk.sync
        def _(sy):
            s_x.inc(sy.dma_start(out=xt[:], in_=x_in[0:128, :]), 16)
            for dst, src in ((wgs, wg_in), (gvec, qgp_in),
                             (gvec2, kgp_in), (idf, idf_in)):
                s_sm.inc(sy.dma_start(out=dst[:], in_=src[:, :]), 16)

        @blk.gpsimd
        def _(g):
            s_w.inc(g.dma_start(out=idb[:], in_=idf_in[:, :]), 16)
            s_w.inc(g.dma_start(out=lngs[:], in_=lng_in[:, :]), 16)
            s_w.inc(g.dma_start(out=lnbs[:], in_=lnb_in[:, :]), 16)
            s_w.inc(g.dma_start(out=bvec[:], in_=qbp_in[:, :]), 16)
            s_w.inc(g.dma_start(out=bvec2[:], in_=kbp_in[:, :]), 16)
            s_g.inc(g.iota(iota8[:], pattern=[[1, 8]], base=0, channel_multiplier=0,
                           allow_small_or_imprecise_dtypes=True))
            for dc in range(NDC):
                s_w.inc(g.dma_start(out=wqs[:, dc * D:(dc + 1) * D],
                                    in_=wq_in[dc * 128:(dc + 1) * 128, :]), 16)

        @blk.vector
        def _(v):
            v.memset(cst[:, 0:1], EPS)
            v.memset(ones_f[:], 1.0)
            v.memset(ones8[:], 1.0)
            v.memset(ones_b1[:], 1.0)
            v.memset(run8[:], 0.0)
            for h in range(H):
                o = h * VW
                v.memset(vst[:, o + 129:o + VW], 0.0)
                v.memset(vst[:, o + 128:o + 129], 1.0)
            s_v.inc(v.engine_nop())

        evacB, evacC, logmm = {}, {}, {}
        xtile_free = {}
        omil = {}
        for tc in range(NTC):
            par = tc % 2

            @blk.vector
            def _(v, tc=tc, par=par):
                v.wait_ge(m_x, 16 * (tc + 1))
                xtile = xt[:]
                v.reduce_sum(sv[:, 0:1], xtile, AX.X).drain()
                s_v.inc(v.tensor_scalar_mul(sv[:, 1:2], sv[:, 0:1], 1.0 / D))

            @blk.scalar
            def _(s, tc=tc, par=par):
                s.wait_ge(m_x, 16 * (tc + 1))
                if tc >= 1:
                    s.wait_ge(m_v, s_v.n)
                xtile = xt[:]
                s_s.inc(s.activation(qrot[:], xtile, AF.Square, accum_out=sv[:, 3:4]))

            @blk.vector
            def _(v, tc=tc):
                v.wait_ge(m_s, s_s.n)
                v.tensor_scalar_mul(sv[:, 4:5], sv[:, 3:4], 1.0 / D).drain()
                v.tensor_scalar(sv[:, 2:3], sv[:, 1:2], sv[:, 1:2], None, ALU.mult).drain()
                s_v.inc(v.tensor_tensor(sv[:, 5:6], sv[:, 4:5], sv[:, 2:3], ALU.subtract))

            @blk.scalar
            def _(s, tc=tc):
                s.wait_ge(m_v, s_v.n)
                s.wait_ge(m_sm, s_sm.n)
                s.activation(sv[:, 6:7], sv[:, 5:6], AF.Ln, bias=cst[:, 0:1]).drain()
                s_s.inc(s.activation(sv[:, 7:8], sv[:, 6:7], AF.Exp, scale=-0.5))

            @blk.vector
            def _(v, tc=tc, par=par):
                v.wait_ge(m_s, s_s.n)
                if ('xntm', tc - 1) in omil:
                    v.wait_ge(m_oxn, omil[('xntm', tc - 1)])
                xtile = xt[:]
                v.tensor_scalar(xn32[:], xtile, sv[:, 1:2], sv[:, 7:8],
                                ALU.subtract, ALU.mult).drain()
                s_v.inc(v.tensor_copy(xnbf[:], xn32[:]))
                xtile_free[tc] = s_v.n
                v.reduce_sum(sv[:, 8:9], xn32[:], AX.X).drain()
                v.tensor_scalar_mul(sv[:, 9:10], sv[:, 8:9], 1.0 / D).drain()
                v.tensor_scalar(sv[:, 10:11], sv[:, 7:8], sv[:, 7:8], None, ALU.mult).drain()
                v.tensor_scalar(sv[:, 11:12], sv[:, 5:6], sv[:, 10:11], None, ALU.mult).drain()
                v.tensor_scalar(sv[:, 12:13], sv[:, 9:10], sv[:, 9:10], None, ALU.mult).drain()
                s_v.inc(v.tensor_tensor(sv[:, 13:14], sv[:, 11:12], sv[:, 12:13], ALU.subtract))

            @blk.scalar
            def _(s, tc=tc):
                s.wait_ge(m_v, s_v.n)
                s.activation(sv[:, 14:15], sv[:, 13:14], AF.Ln, bias=cst[:, 0:1]).drain()
                s.activation(sv[:, 15:16], sv[:, 14:15], AF.Exp, scale=-0.5).drain()
                s.activation(sv[:, 16:17], sv[:, 11:12], AF.Ln, scale=float(D)).drain()
                s.activation(sv[:, 17:18], sv[:, 16:17], AF.Exp, scale=0.5).drain()
                s_s.inc(s.mul(msave[:, tc:tc + 1], sv[:, 17:18], SC025))
                omil[('xntm', tc)] = o_xn.inc(
                    s.dma_start(out=xn_tm_o[tc * 128:(tc + 1) * 128, :], in_=xnbf[:]), 16)

            @blk.vector
            def _(v, tc=tc):
                v.wait_ge(m_s, s_s.n)
                s_v.inc(v.tensor_scalar(xnnt[:], xn32[:], sv[:, 9:10], sv[:, 15:16],
                                        ALU.subtract, ALU.mult))

            xnn_ready = s_v.n

            for dc in range(NDC):
                pb = dc % 2

                @blk.tensor
                def _(te, tc=tc, dc=dc, pb=pb):
                    if dc == 0:
                        te.wait_ge(m_v, xnn_ready)
                        te.wait_ge(m_w, s_w.n)
                        te.wait_ge(m_sm, s_sm.n)
                    for key in ((tc, dc - 2), (tc - 1, NDC - 2 + dc) if dc < 2 else None):
                        if key and key in evacB:
                            te.wait_ge(m_s, evacB[key])
                            te.wait_ge(m_v, evacC[key])
                    s_t.inc(te.transpose(psB[:, pb * 512:pb * 512 + 128],
                                         xn32[:, dc * 128:(dc + 1) * 128], idf[:]))
                    s_t.inc(te.transpose(psC[:, pb * 1024:pb * 1024 + 128],
                                         xnnt[:, dc * 128:(dc + 1) * 128], idb[:]))

                @blk.scalar
                def _(s, tc=tc, dc=dc, pb=pb):
                    s.wait_ge(m_t, s_t.n - 1)
                    if (tc, dc - 2) in logmm:
                        s.wait_ge(m_t, logmm[(tc, dc - 2)])
                    evacB[(tc, dc)] = s_s.inc(s.copy(xfm32[:, pb * 128:(pb + 1) * 128],
                                                     psB[:, pb * 512:pb * 512 + 128]))

                @blk.vector
                def _(v, tc=tc, dc=dc, pb=pb):
                    v.wait_ge(m_t, s_t.n)
                    if tc >= 1 and dc == 0:
                        v.wait_ge(m_oxf, omil[('xnfm', tc - 1)])
                    v.tensor_copy(xnn_fm[:, dc * TPC + tc * 128: dc * TPC + (tc + 1) * 128],
                                  psC[:, pb * 1024:pb * 1024 + 128]).drain()
                    v.wait_ge(m_s, evacB[(tc, dc)])
                    evacC[(tc, dc)] = s_v.inc(
                        v.tensor_copy(tstg[:, dc * 128:(dc + 1) * 128],
                                      xfm32[:, pb * 128:(pb + 1) * 128]))

                @blk.tensor
                def _(te, tc=tc, dc=dc, pb=pb):
                    te.wait_ge(m_s, evacB[(tc, dc)])
                    logmm[(tc, dc)] = s_t.inc(
                        te.matmul(pslog[:, :], wgs[:, dc * 8:(dc + 1) * 8],
                                  xfm32[:, pb * 128:(pb + 1) * 128],
                                  start=(dc == 0), stop=(dc == NDC - 1)))

            @blk.sync
            def _(sy, tc=tc):
                sy.wait_ge(m_v, s_v.n)
                omil[('xnfm', tc)] = o_xf.inc(
                    sy.dma_start(out=xn_fm_o[tc],
                                 in_=tstg.rearrange("p (c w) -> p c w", w=128)), 16)
                if tc + 1 < NTC:
                    sy.wait_ge(m_v, xtile_free[tc])
                    s_x.inc(sy.dma_start(out=xt[:],
                                         in_=x_in[(tc + 1) * 128:(tc + 2) * 128, :]), 16)

            @blk.scalar
            def _(s, tc=tc):
                s.wait_ge(m_t, s_t.n)
                s_s.inc(s.copy(logsb[:, :], pslog[:, :]))

            @blk.tensor
            def _(te, tc=tc):
                te.wait_ge(m_s, s_s.n)
                s_t.inc(te.transpose(psB[:, 256:264], logsb[:, :], idf[0:8, 0:8]))

            @blk.vector
            def _(v, tc=tc):
                v.wait_ge(m_t, s_t.n)
                v.tensor_copy(ltm[:], psB[:, 256:264]).drain()
                v.reduce_max(sv[:, 19:20], ltm[:], AX.X).drain()
                s_v.inc(v.tensor_scalar_mul(sv[:, 20:21], sv[:, 19:20], -1.0))

            @blk.scalar
            def _(s, tc=tc):
                s.wait_ge(m_v, s_v.n)
                s_s.inc(s.activation(probs[:], ltm[:], AF.Exp, bias=sv[:, 20:21],
                                     accum_out=sv[:, 21:22]))

            @blk.vector
            def _(v, tc=tc):
                v.wait_ge(m_s, s_s.n)
                v.wait_ge(m_g, 1)
                v.reciprocal(sv[:, 23:24], sv[:, 21:22]).drain()
                v.tensor_scalar(oneh[:], ltm[:], sv[:, 19:20], None, ALU.is_equal).drain()
                v.tensor_tensor(t8[:], oneh[:], iota8[:], ALU.mult).drain()
                s_v.inc(v.reduce_sum(sv[:, 24:25], t8[:], AX.X))

            @blk.tensor
            def _(te, tc=tc):
                te.wait_ge(m_v, s_v.n)
                s_t.inc(te.matmul(pscnt[:, :], oneh[:], ones_f[:],
                                  start=(tc == 0), stop=(tc == NTC - 1)))
                s_t.inc(te.transpose(psB[0:8, 640:768], oneh[:], idf[:]))

            @blk.vector
            def _(v, tc=tc):
                v.wait_ge(m_t, s_t.n)
                v.tensor_copy(ohT[:], psB[0:8, 640:768]).drain()
                v.tensor_tensor_scan(scn[:], ohT[:], ohT[:], run8[:, 0:1],
                                     op0=ALU.add, op1=ALU.bypass).drain()
                v.tensor_copy(run8[:], scn[:, 127:128]).drain()
                s_v.inc(v.tensor_tensor(sel[:], ohT[:], scn[:], ALU.mult))

            @blk.tensor
            def _(te, tc=tc):
                te.wait_ge(m_v, s_v.n)
                s_t.inc(te.matmul(psB[0:128, 256:257], sel[:], ones8[:],
                                  start=True, stop=True))

            @blk.vector
            def _(v, tc=tc):
                v.wait_ge(m_t, s_t.n)
                v.tensor_scalar(colst[:, tc:tc + 1],
                                psB[0:128, 256:257], -1.0, None, ALU.add).drain()
                v.tensor_copy(colst[:, NTC + tc:NTC + tc + 1], sv[:, 24:25]).drain()
                s_v.inc(v.tensor_copy(colst[:, 2 * NTC + tc:2 * NTC + tc + 1],
                                      sv[:, 23:24]))

        @blk.scalar
        def _(s):
            s.wait_ge(m_v, s_v.n)
            s.wait_ge(m_t, s_t.n)
            s.copy(cnt8[:], pscnt[:, :]).drain()
            s_o.inc(s.dma_start(out=cnt_o[:, :], in_=cnt8[:]), 16)
            s_o.inc(s.dma_start(out=pos_o[:, :], in_=colst[:, 0:NTC]), 16)
            s_o.inc(s.dma_start(out=eidx_o[:, :], in_=colst[:, NTC:2 * NTC]), 16)
            s_o.inc(s.dma_start(out=gate_o[:, :], in_=colst[:, 2 * NTC:3 * NTC]), 16)

        state = {"qrow_free": {}, "qnb_free": {}, "rope_ready": {}, "rope_free": {}}

        def proj_pass(brO_dram, brO_off, brow_ap, gv, bv, post, tag):
            ev_b = {}
            for oc in range(4):
                @blk.sync
                def _(sy, oc=oc):
                    if oc >= 1:
                        sy.wait_ge(m_v, ev_b[oc - 1])
                    s_br.inc(sy.dma_start(out=brtmp[0:1, :],
                                          in_=brO_dram[0:1, brO_off + oc * 512:brO_off + (oc + 1) * 512]), 16)

                @blk.tensor
                def _(te, oc=oc):
                    if oc == 0:
                        te.wait_ge(m_w, s_w.n)
                        te.wait_ge(m_sm, s_sm.n)
                    if oc >= 2:
                        te.wait_ge(m_v, ev_b[oc - 2])
                    for dc in range(NDC):
                        mm = te.matmul(psA[0:1, (oc % 2) * 512:(oc % 2) * 512 + 512],
                                       bv[:, dc:dc + 1],
                                       wqs[:, dc * D + oc * 512:dc * D + (oc + 1) * 512],
                                       start=(dc == 0), stop=(dc == NDC - 1))
                    s_t.inc(mm)

                @blk.vector
                def _(v, oc=oc):
                    v.wait_ge(m_t, s_t.n)
                    v.wait_ge(m_br, s_br.n)
                    ev_b[oc] = s_v.inc(v.tensor_tensor(
                        brow_ap[0:1, oc * 512:(oc + 1) * 512],
                        psA[0:1, (oc % 2) * 512:(oc % 2) * 512 + 512],
                        brtmp[0:1, :], ALU.add))

            @blk.vector
            def _(v):
                v.wait_ge(m_t, s_t.n)
                for dc in range(NDC):
                    v.tensor_scalar(wqs[:, dc * D:(dc + 1) * D], wqs[:, dc * D:(dc + 1) * D],
                                    gv[:, dc:dc + 1], None, ALU.mult)
                    v.drain()
                s_v.inc(v.engine_nop())
            w_scaled = s_v.n

            ev_q = {}
            for tc in range(NTC):
                for oc in range(4):
                    @blk.tensor
                    def _(te, tc=tc, oc=oc):
                        if tc == 0 and oc == 0:
                            te.wait_ge(m_v, w_scaled)
                        k = tc * 4 + oc
                        if k >= 2:
                            te.wait_ge(m_s, ev_q[k - 2])
                        for dc in range(NDC):
                            te.matmul(psA[:, (k % 2) * 512:(k % 2) * 512 + 512],
                                      xnn_fm[:, dc * TPC + tc * 128: dc * TPC + (tc + 1) * 128],
                                      wqs[:, dc * D + oc * 512:dc * D + (oc + 1) * 512],
                                      start=(dc == 0), stop=False)
                        mm = te.matmul(psA[:, (k % 2) * 512:(k % 2) * 512 + 512],
                                       ones_b1[:], brow_ap[0:1, oc * 512:(oc + 1) * 512],
                                       start=False, stop=True)
                        s_t.inc(mm)

                    @blk.scalar
                    def _(s, tc=tc, oc=oc, tag=tag):
                        s.wait_ge(m_t, s_t.n)
                        if oc == 0 and (tag, tc - 1) in state["qrow_free"]:
                            s.wait_ge(m_v, state["qrow_free"][(tag, tc - 1)])
                        k = tc * 4 + oc
                        if tag == 'v':
                            if oc == 0 and tc >= 1:
                                s.wait_ge(m_ov, omil[('v', tc - 1)])
                            for j in range(4):
                                h = oc * 4 + j
                                cp = s.copy(vst[:, h * VW:h * VW + 128],
                                            psA[:, (k % 2) * 512 + j * 128:(k % 2) * 512 + (j + 1) * 128])
                            ev_q[k] = s_s.inc(cp)
                        else:
                            ev_q[k] = s_s.inc(s.copy(qrow[:, oc * 512:(oc + 1) * 512],
                                                     psA[:, (k % 2) * 512:(k % 2) * 512 + 512]))
                post(tc)

        def mk_rope_ln_post(out_dram, tag):
            def post(tc):
                par = tc % 2

                @blk.vector
                def _(v, tc=tc, par=par, tag=tag):
                    v.wait_ge(m_s, s_s.n)
                    v.wait_ge(m_rope, state["rope_ready"][(tag, tc)])
                    qv = qrow.rearrange("p (hi two) -> p hi two", two=2)
                    ov = qrot.rearrange("p (hi two) -> p hi two", two=2)
                    ctv = ct[:, :]
                    stv = st[:, :]
                    v.tensor_tensor(rs1[:], qv[:, :, 0], ctv, ALU.mult)
                    v.tensor_tensor(rs2[:], qv[:, :, 1], stv, ALU.mult).drain()
                    v.tensor_tensor(ov[:, :, 0], rs1[:], rs2[:], ALU.subtract).drain()
                    v.tensor_tensor(rs1[:], qv[:, :, 0], stv, ALU.mult)
                    v.tensor_tensor(rs2[:], qv[:, :, 1], ctv, ALU.mult).drain()
                    s_v.inc(v.tensor_tensor(ov[:, :, 1], rs1[:], rs2[:], ALU.add))
                    state["qrow_free"][(tag, tc)] = s_v.n
                    state["rope_free"][(tag, tc)] = s_v.n
                    v.drain()
                    v.tensor_scalar(qrot[:], qrot[:], msave[:, tc:tc + 1], None, ALU.mult).drain()
                    qr3 = qrot.rearrange("p (h w) -> p h w", w=128)
                    s_v.inc(v.reduce_sum(hstat[:, 0:H], qr3, AX.X))

                @blk.sync
                def _(sy, tc=tc, tag=tag):
                    if tc + 1 < NTC:
                        sy.wait_ge(m_v, state["rope_free"][(tag, tc)])
                        sy.dma_start(out=ct[:, :],
                                     in_=cos_in[(tc + 1) * 128:(tc + 2) * 128, :]).then_inc(m_rope, 16)
                        s_rope.n += 16
                        sy.dma_start(out=st[:, :],
                                     in_=sin_in[(tc + 1) * 128:(tc + 2) * 128, :]).then_inc(m_rope, 16)
                        s_rope.n += 16
                        state["rope_ready"][(tag, tc + 1)] = s_rope.n

                @blk.scalar
                def _(s, tc=tc):
                    s.wait_ge(m_v, s_v.n)
                    s_s.inc(s.activation(qrow[:], qrot[:], AF.Square))

                @blk.vector
                def _(v, tc=tc):
                    v.wait_ge(m_s, s_s.n)
                    sq3 = qrow.rearrange("p (h w) -> p h w", w=128)
                    v.reduce_sum(hstat[:, H:2 * H], sq3, AX.X).drain()
                    v.tensor_scalar_mul(hstat[:, 2 * H:3 * H], hstat[:, 0:H], 1.0 / HD).drain()
                    v.tensor_scalar_mul(hstat[:, H:2 * H], hstat[:, H:2 * H], 1.0 / HD).drain()
                    v.tensor_tensor(hstat[:, 3 * H:4 * H], hstat[:, 2 * H:3 * H],
                                    hstat[:, 2 * H:3 * H], ALU.mult).drain()
                    s_v.inc(v.tensor_tensor(hstat[:, H:2 * H], hstat[:, H:2 * H],
                                            hstat[:, 3 * H:4 * H], ALU.subtract))

                @blk.scalar
                def _(s, tc=tc):
                    s.wait_ge(m_v, s_v.n)
                    s.activation(hstat[:, 3 * H:4 * H], hstat[:, H:2 * H], AF.Ln,
                                 bias=cst[:, 0:1]).drain()
                    s_s.inc(s.activation(hstat[:, H:2 * H], hstat[:, 3 * H:4 * H],
                                         AF.Exp, scale=-0.5))

                @blk.vector
                def _(v, tc=tc, tag=tag):
                    v.wait_ge(m_s, s_s.n)
                    if (tag, tc - 1) in state["qnb_free"]:
                        v.wait_ge(m_t, state["qnb_free"][(tag, tc - 1)])
                    for h in range(H):
                        v.tensor_scalar(qnb[:, h * 128:(h + 1) * 128],
                                        qrot[:, h * 128:(h + 1) * 128],
                                        hstat[:, 2 * H + h:2 * H + h + 1],
                                        hstat[:, H + h:H + h + 1],
                                        ALU.subtract, ALU.mult)
                    v.drain()
                    v.tensor_tensor(qnb[:], qnb[:], lngs[:], ALU.mult).drain()
                    s_v.inc(v.tensor_tensor(qnb[:], qnb[:], lnbs[:], ALU.add))

                trev = {}
                for hc in range(H):
                    @blk.tensor
                    def _(te, tc=tc, hc=hc):
                        if hc == 0:
                            te.wait_ge(m_v, s_v.n)
                        if hc - 2 in trev:
                            te.wait_ge(m_s, trev[hc - 2])
                        s_t.inc(te.transpose(psC[:, (hc % 2) * 1024:(hc % 2) * 1024 + 128],
                                             qnb[:, hc * 128:(hc + 1) * 128], idb[:]))

                    @blk.scalar
                    def _(s, tc=tc, hc=hc, tag=tag):
                        s.wait_ge(m_t, s_t.n)
                        if hc == 0 and tc == 0:
                            s.wait_ge(m_oxf, o_xf.n)
                            if tag == 'k':
                                s.wait_ge(m_oq, o_tag['q'].n)
                        if hc == 0 and tc >= 1:
                            s.wait_ge(o_sem[tag], omil[(tag, tc - 1)])
                        trev[hc] = s_s.inc(
                            s.copy(tstg[:, hc * 128:(hc + 1) * 128],
                                   psC[:, (hc % 2) * 1024:(hc % 2) * 1024 + 128]))
                state["qnb_free"][(tag, tc)] = s_t.n

                @blk.sync
                def _(sy, tc=tc, tag=tag):
                    sy.wait_ge(m_s, s_s.n)
                    omil[(tag, tc)] = o_tag[tag].inc(
                        sy.dma_start(out=out_dram[tc],
                                     in_=tstg.rearrange("p (c w) -> p c w", w=128)), 16)
            return post

        @blk.sync
        def _(sy):
            sy.dma_start(out=ct[:, :], in_=cos_in[0:128, :]).then_inc(m_rope, 16)
            s_rope.n += 16
            sy.dma_start(out=st[:, :], in_=sin_in[0:128, :]).then_inc(m_rope, 16)
            s_rope.n += 16
        state["rope_ready"][('q', 0)] = s_rope.n
        proj_pass(bqrow_in, 0, brow, gvec, bvec, mk_rope_ln_post(qnT_o, 'q'), 'q')

        @blk.gpsimd
        def _(g):
            g.wait_ge(m_t, s_t.n)
            for dc in range(NDC):
                s_w.inc(g.dma_start(out=wqs[:, dc * D:(dc + 1) * D],
                                    in_=wkv_in[dc * 128:(dc + 1) * 128, 0:D]), 16)

        @blk.sync
        def _(sy):
            sy.wait_ge(m_v, s_v.n)
            sy.dma_start(out=ct[:, :], in_=cos_in[0:128, :]).then_inc(m_rope, 16)
            s_rope.n += 16
            sy.dma_start(out=st[:, :], in_=sin_in[0:128, :]).then_inc(m_rope, 16)
            s_rope.n += 16
        state["rope_ready"][('k', 0)] = s_rope.n
        proj_pass(bkvrow_in, 0, brow[0:1, 0:D], gvec2, bvec2,
                  mk_rope_ln_post(knT_o, 'k'), 'k')

        @blk.gpsimd
        def _(g):
            g.wait_ge(m_t, s_t.n)
            for dc in range(NDC):
                s_w.inc(g.dma_start(out=wqs[:, dc * D:(dc + 1) * D],
                                    in_=wkv_in[dc * 128:(dc + 1) * 128, D:2 * D]), 16)

        def v_post(tc):
            @blk.sync
            def _(sy, tc=tc):
                sy.wait_ge(m_s, s_s.n)
                omil[('v', tc)] = o_tag['v'].inc(
                    sy.dma_start(out=vch_o[tc],
                                 in_=vst.rearrange("p (c w) -> p c w", w=VW)), 16)

        proj_pass(bkvrow_in, D, brow[0:1, 0:D], gvec2, bvec2, v_post, 'v')

        @blk.sync
        def _(sy):
            sy.wait_ge(m_o, s_o.n)
            sy.wait_ge(m_oxn, o_xn.n)
            sy.wait_ge(m_oxf, o_xf.n)
            sy.wait_ge(m_orow, o_row.n)
            sy.wait_ge(m_oq, o_tag['q'].n)
            sy.wait_ge(m_ok, o_tag['k'].n)
            sy.wait_ge(m_ov, o_tag['v'].n)
    return nc



def _pmajor(vec):
    return np.ascontiguousarray(np.asarray(vec, np.float32).reshape(NDC, 128).T)


def pre0_inputs(full):
    x = np.asarray(full["L0_attn_x"], np.float32).reshape(T, D)
    wq = np.asarray(full["L0_attn_Wq"], np.float32)
    wkv = np.asarray(full["L0_attn_Wkv"], np.float32)
    bq = np.asarray(full["L0_attn_bq"], np.float32)
    bkv = np.asarray(full["L0_attn_bkv"], np.float32)
    qg = np.asarray(full["L0_attn_qn_g"], np.float32)
    qb = np.asarray(full["L0_attn_qn_b"], np.float32)
    kg = np.asarray(full["L0_attn_kvn_g"], np.float32)
    kb = np.asarray(full["L0_attn_kvn_b"], np.float32)
    lng = np.asarray(full["L0_attn_ln_g"], np.float32)
    lnb = np.asarray(full["L0_attn_ln_b"], np.float32)
    wg = np.asarray(full["L0_ffn_Wg"], np.float32)

    pos = np.arange(SEQ, dtype=np.float64)
    freqs = (np.arange(0, HD, 2, dtype=np.float64) / HD) * np.log(30000.0)
    ang = pos[:, None] * freqs[None, :]
    cosf = np.cos(ang).astype(np.float32)
    sinf = np.sin(ang).astype(np.float32)
    cos_t = np.tile(cosf, (1, H))
    sin_t = np.tile(sinf, (1, H))
    ident = np.eye(128, dtype=np.float32)
    lng_full = np.broadcast_to(np.tile(lng, H)[None, :], (128, D)).copy()
    lnb_full = np.broadcast_to(np.tile(lnb, H)[None, :], (128, D)).copy()
    wg_p = np.ascontiguousarray(wg.reshape(NDC, 128, E).transpose(1, 0, 2)).reshape(128, NDC * E)

    maps = []
    for c in range(NCORE):
        t0 = c * TPC
        posc = (np.arange(t0, t0 + TPC) % SEQ)
        maps.append({
            "x": np.ascontiguousarray(x[t0:t0 + TPC]),
            "wq": wq, "wkv": wkv,
            "bqrow": np.asarray(bq, np.float32)[None, :],
            "bkvrow": np.asarray(bkv, np.float32)[None, :],
            "qbp": _pmajor(qb), "qgp": _pmajor(qg),
            "kbp": _pmajor(kb), "kgp": _pmajor(kg),
            "wg": wg_p,
            "lng": lng_full, "lnb": lnb_full,
            "cosT": np.ascontiguousarray(cos_t[posc]),
            "sinT": np.ascontiguousarray(sin_t[posc]),
            "identf": ident,
        })
    return maps



def build_attn0(seq=SEQ, parts=7):
    nc = bass.Bass("TRN2", target_bir_lowering=False)
    ap = lambda t: t.ap()
    NP = 4
    KC = seq // 128
    NQT = seq // 512
    SCL = 1.0 / float(np.sqrt(HD))
    qnT_in = ap(nc.dram_tensor("qnT", [KC, 128, NP, 128], BF16, kind="ExternalInput"))
    knT_in = ap(nc.dram_tensor("knT", [KC, 128, NP, 128], BF16, kind="ExternalInput"))
    vch_in = ap(nc.dram_tensor("vch", [KC, 128, NP, VW], BF16, kind="ExternalInput"))
    cnts_in = ap(nc.dram_tensor("cnts", [8, 8], F32, kind="ExternalInput"))
    cmask_in = ap(nc.dram_tensor("cmask", [128, 2048], F32, kind="ExternalInput"))
    eidx_in = ap(nc.dram_tensor("eidx", [128, NTC], F32, kind="ExternalInput"))
    pos_in = ap(nc.dram_tensor("pos", [128, NTC], F32, kind="ExternalInput"))
    gate_in = ap(nc.dram_tensor("gate", [128, NTC], F32, kind="ExternalInput"))
    tokid_in = ap(nc.dram_tensor("tokid", [128, NTC], F32, kind="ExternalInput"))
    pmask_in = ap(nc.dram_tensor("pmask", [8, 1], F32, kind="ExternalInput"))
    idf_in = ap(nc.dram_tensor("identf", [128, 128], F32, kind="ExternalInput"))

    attn_o = ap(nc.dram_tensor("attn", [KC, 128, NP * 128], BF16, kind="ExternalOutput"))
    srcp_o = ap(nc.dram_tensor("srcp", [CAP * E + 128, 1], F32, kind="ExternalOutput"))
    flatc_o = ap(nc.dram_tensor("flatc", [128, NTC], F32, kind="ExternalOutput"))
    gatek_o = ap(nc.dram_tensor("gatek", [128, NTC], F32, kind="ExternalOutput"))
    TRASH = float(CAP * E + 64)

    with ExitStack() as _ctx:
        ks = _ctx.enter_context(nc.sbuf_tensor("ks", [128, KC * 128], BF16))
        vs = _ctx.enter_context(nc.sbuf_tensor("vs", [128, KC * VW], BF16))
        qsall = _ctx.enter_context(nc.sbuf_tensor("qsall", [128, KC * 128], BF16))
        ps = _ctx.enter_context(nc.sbuf_tensor("ps", [128, 2 * 1024], BF16))
        sq2s = _ctx.enter_context(nc.sbuf_tensor("sq2s", [128, 512], F32))
        cmk = _ctx.enter_context(nc.sbuf_tensor("cmk", [128, 2048], BF16))
        ev = _ctx.enter_context(nc.sbuf_tensor("ev", [128, 4 * 132], F32))
        ost = _ctx.enter_context(nc.sbuf_tensor("ost", [128, 512], BF16))
        bcol = _ctx.enter_context(nc.sbuf_tensor("bcol", [128, 1], F32))
        svr = _ctx.enter_context(nc.sbuf_tensor("svr", [1, 520], F32))
        idf = _ctx.enter_context(nc.sbuf_tensor("idf", [128, 128], F32))
        ones_f = _ctx.enter_context(nc.sbuf_tensor("ones_f", [128, 1], F32))
        onesb = _ctx.enter_context(nc.sbuf_tensor("onesb", [1, 128], F32))
        cnts = _ctx.enter_context(nc.sbuf_tensor("cntsS", [8, 8], F32))
        pmask = _ctx.enter_context(nc.sbuf_tensor("pmaskS", [8, 1], F32))
        base8c = _ctx.enter_context(nc.sbuf_tensor("base8c", [8, 1], F32))
        base8 = _ctx.enter_context(nc.sbuf_tensor("base8", [1, 8], F32))
        erow = _ctx.enter_context(nc.sbuf_tensor("erow", [128, NTC], F32))
        prow = _ctx.enter_context(nc.sbuf_tensor("prow", [128, NTC], F32))
        grow = _ctx.enter_context(nc.sbuf_tensor("grow", [128, NTC], F32))
        brow_ = _ctx.enter_context(nc.sbuf_tensor("brow_", [128, NTC], F32))
        flat = _ctx.enter_context(nc.sbuf_tensor("flat", [128, NTC], F32))
        keep = _ctx.enter_context(nc.sbuf_tensor("keep", [128, NTC], F32))
        tmpr = _ctx.enter_context(nc.sbuf_tensor("tmpr", [128, NTC], F32))
        base8b = _ctx.enter_context(nc.sbuf_tensor("base8b", [128, 8], F32))
        flt2 = _ctx.enter_context(nc.sbuf_tensor("flt2", [128, NTC], F32))
        tokid = _ctx.enter_context(nc.sbuf_tensor("tokidS", [128, NTC], F32))
        fcol = _ctx.enter_context(nc.sbuf_tensor("fcol", [128, NTC], F32))
        fci = _ctx.enter_context(nc.sbuf_tensor("fci", [128, NTC], I32))
        sent = _ctx.enter_context(nc.sbuf_tensor("sent", [128, 130], F32))
        psS = _ctx.enter_context(nc.psum_tensor("psS", [128, 2048], F32))
        av0 = _ctx.enter_context(nc.psum_tensor("av0", [128, 512], F32))
        av1 = _ctx.enter_context(nc.psum_tensor("av1", [128, 512], F32))
        av2 = _ctx.enter_context(nc.psum_tensor("av2", [128, 512], F32))
        av3 = _ctx.enter_context(nc.psum_tensor("av3", [128, 512], F32))
        m_kv = _ctx.enter_context(nc.semaphore("m_kv"))
        m_sm = _ctx.enter_context(nc.semaphore("m_sm"))
        m_t = _ctx.enter_context(nc.semaphore("m_t"))
        m_v = _ctx.enter_context(nc.semaphore("m_v"))
        m_s = _ctx.enter_context(nc.semaphore("m_s"))
        m_g = _ctx.enter_context(nc.semaphore("m_g"))
        m_o = _ctx.enter_context(nc.semaphore("m_o"))
        m_oa = _ctx.enter_context(nc.semaphore("m_oa"))
        m_qk = _ctx.enter_context(nc.semaphore("m_qk"))
        m_ex = _ctx.enter_context(nc.semaphore("m_ex"))
        m_mk = _ctx.enter_context(nc.semaphore("m_mk"))
        m_osc = _ctx.enter_context(nc.semaphore("m_osc"))
        m_av = _ctx.enter_context(nc.semaphore("m_av"))
        m_ev = _ctx.enter_context(nc.semaphore("m_ev"))
        blk = _BlkW(_ctx.enter_context(nc.Block()))

        s_kv, s_sm, s_t = SC(m_kv), SC(m_sm), SC(m_t)
        s_v, s_s, s_g = SC(m_v), SC(m_s), SC(m_g)
        s_o, s_oa = SC(m_o), SC(m_oa)
        s_qk, s_ex, s_mk, s_av, s_ev = SC(m_qk), SC(m_ex), SC(m_mk), SC(m_av), SC(m_ev)
        av = [av0, av1, av2, av3]

        @blk.sync
        def _(sy):
            for dst, src in ((cnts, cnts_in), (pmask, pmask_in), (idf, idf_in),
                             (erow, eidx_in), (prow, pos_in), (grow, gate_in),
                             (tokid, tokid_in)):
                s_sm.inc(sy.dma_start(out=dst[:], in_=src[:, :]), 16)

        @blk.gpsimd
        def _(g):
            s_g.inc(g.dma_start(out=cmk[:], in_=cmask_in[:, :]), 16)

        @blk.vector
        def _(v):
            v.memset(ones_f[:], 1.0)
            v.memset(onesb[:], 1.0)
            s_v.inc(v.memset(sent[:], 1.0e9))

        do_routing = bool(parts & 1)
        do_attn = bool(parts & 2)
        do_scatter = bool(parts & 4)
        s_osc = SC(m_osc)
        if do_routing:
          @blk.tensor
          def _(te):
            te.wait_ge(m_sm, s_sm.n)
            s_t.inc(te.matmul(av0[0:8, 0:1], cnts[:, :], pmask[:, :],
                              start=True, stop=True))

          @blk.vector
          def _(v):
              v.wait_ge(m_t, s_t.n)
              s_v.inc(v.tensor_copy(base8c[:], av0[0:8, 0:1]))

          @blk.tensor
          def _(te):
              te.wait_ge(m_v, s_v.n)
              s_t.inc(te.transpose(av0[0:1, 128:136], base8c[:], idf[0:8, 0:8]))

          @blk.vector
          def _(v):
              v.wait_ge(m_t, s_t.n)
              s_v.inc(v.tensor_copy(base8[:], av0[0:1, 128:136]))

          @blk.tensor
          def _(te):
              te.wait_ge(m_v, s_v.n)
              s_t.inc(te.matmul(av1[0:128, 192:200], onesb[:, :], base8[:, :],
                                start=True, stop=True))

          @blk.vector
          def _(v):
              v.wait_ge(m_t, s_t.n)
              v.tensor_copy(base8b[:], av1[0:128, 192:200]).drain()
              v.memset(brow_[:], 0.0)
              v.drain()
              for e in range(E):
                  v.tensor_scalar(tmpr[:], erow[:], float(e), None, ALU.is_equal)
                  v.drain()
                  v.scalar_tensor_tensor(brow_[:], tmpr[:], base8b[:, e:e + 1],
                                         brow_[:], ALU.mult, ALU.add)
                  v.drain()
              v.tensor_tensor(prow[:], prow[:], brow_[:], ALU.add).drain()
              v.tensor_scalar(keep[:], prow[:], float(CAP), None, ALU.is_lt).drain()
              v.tensor_scalar(flat[:], erow[:], float(CAP), None, ALU.mult).drain()
              v.tensor_tensor(flat[:], flat[:], prow[:], ALU.add).drain()
              v.tensor_scalar(flt2[:], flat[:], float(E * CAP - 1), None, ALU.min).drain()
              v.tensor_tensor(grow[:], grow[:], keep[:], ALU.mult).drain()
              v.tensor_tensor(flat[:], flat[:], keep[:], ALU.mult).drain()
              v.tensor_scalar(tmpr[:], keep[:], -1.0, 1.0, ALU.mult, ALU.add).drain()
              v.scalar_tensor_tensor(flat[:], tmpr[:], TRASH, flat[:], ALU.mult, ALU.add)
              s_v.inc(v.engine_nop())

          @blk.scalar
          def _(s):
              s.wait_ge(m_v, s_v.n)
              s_o.inc(s.dma_start(out=flatc_o[:, :], in_=flt2[:]), 16)
              s_o.inc(s.dma_start(out=gatek_o[:, :], in_=grow[:]), 16)

          @blk.vector
          def _(v):
              if do_scatter:
                  v.wait_ge(m_v, s_v.n)
                  v.wait_ge(m_s, s_s.n)
                  s_v.inc(v.tensor_copy(fci[:], flat[:]))
                  v.wait_ge(m_v, s_v.n)
                  s_v.inc(v.tensor_copy(tmpr[:, 0:1], fci[:, 0:1]))

          @blk.gpsimd
          def _(g):
              if not do_scatter or (parts & 8):
                  return
              g.wait_ge(m_v, s_v.n)
              g.dma_start(out=srcp_o.rearrange("(p c) w -> p (c w)", p=128),
                          in_=sent[:, 0:129]).then_inc(m_osc, 16)
              s_osc.n += 16
              g.wait_ge(m_osc, s_osc.n)
              for tc in range(NTC):
                  g.indirect_dma_start(
                      srcp_o[:, :], IndirectOffsetOnAxis(fci[:, tc:tc + 1], 0),
                      tokid[:, tc:tc + 1], None,
                  ).then_inc(m_osc, 16)
                  s_osc.n += 16
                  g.wait_ge(m_osc, s_osc.n)
        for pr in range(NP if do_attn else 0):
            prev_ev = s_ev.n
            prev_ex = s_ex.n
            prev_av = s_av.n
            prev_trv = s_v.n
            @blk.sync
            def _(sy, pr=pr):
                if pr >= 1:
                    sy.wait_ge(m_qk, s_qk.n)
                    sy.wait_ge(m_av, s_av.n)
                for kc in range(KC):
                    sy.dma_start(out=ks[:, kc * 128:(kc + 1) * 128],
                                 in_=knT_in[kc, :, pr, :]).then_inc(m_kv, 16)
                    s_kv.n += 16
                    sy.dma_start(out=vs[:, kc * VW:(kc + 1) * VW],
                                 in_=vch_in[kc, :, pr, :]).then_inc(m_kv, 16)
                    s_kv.n += 16
                    sy.dma_start(out=qsall[:, kc * 128:(kc + 1) * 128],
                                 in_=qnT_in[kc, :, pr, :]).then_inc(m_kv, 16)
                    s_kv.n += 16

            @blk.vector
            def _(v, pr=pr):
                v.memset(svr[0:1, 512:513], 0.0)
                s_v.inc(v.memset(svr[0:1, 513:514], 0.0))

            for which, buf in (("k", ks), ("q", qsall)):
                for j in range(NQT):
                    @blk.scalar
                    def _(s, j=j, buf=buf, pr=pr, which=which):
                        if j == 0 and which == "k":
                            s.wait_ge(m_kv, s_kv.n)
                        else:
                            s.wait_ge(m_t, s_t.n)
                            s.wait_ge(m_s, s_s.n)
                        s_s.inc(s.activation(sq2s[:], buf[:, j * 512:(j + 1) * 512],
                                             AF.Square))

                    @blk.tensor
                    def _(te, j=j, pr=pr, which=which, prev_ev=prev_ev):
                        te.wait_ge(m_s, s_s.n)
                        if j == 0 and which == "k":
                            te.wait_ge(m_ev, prev_ev)
                            te.wait_ge(m_v, prev_trv)
                        else:
                            te.wait_ge(m_v, s_v.n)
                        s_t.inc(te.matmul(av2[0:1, 0:512], ones_f[:], sq2s[:],
                                          start=True, stop=True))

                    @blk.vector
                    def _(v, j=j, which=which):
                        v.wait_ge(m_t, s_t.n)
                        s_v.inc(v.reduce_max(svr[0:1, 514:515], av2[0:1, 0:512], AX.X))
                        v.wait_ge(m_v, s_v.n)
                        col = 512 if which == "k" else 513
                        s_v.inc(v.tensor_tensor(svr[0:1, col:col + 1], svr[0:1, col:col + 1],
                                                svr[0:1, 514:515], ALU.max))

            @blk.vector
            def _(v, pr=pr):
                v.wait_ge(m_v, s_v.n)
                s_v.inc(v.tensor_tensor(svr[0:1, 515:516], svr[0:1, 512:513],
                                        svr[0:1, 513:514], ALU.mult))

            @blk.scalar
            def _(s, pr=pr):
                s.wait_ge(m_v, s_v.n)
                s_s.inc(s.activation(svr[0:1, 516:517], svr[0:1, 515:516], AF.Sqrt))

            @blk.vector
            def _(v, pr=pr):
                v.wait_ge(m_s, s_s.n)
                s_v.inc(v.tensor_scalar_mul(svr[0:1, 517:518], svr[0:1, 516:517], -SCL))

            @blk.tensor
            def _(te, pr=pr):
                te.wait_ge(m_v, s_v.n)
                s_t.inc(te.matmul(av3[0:128, 0:1], onesb[:, :], svr[0:1, 517:518],
                                  start=True, stop=True))

            @blk.vector
            def _(v, pr=pr):
                v.wait_ge(m_t, s_t.n)
                s_v.inc(v.tensor_copy(bcol[:], av3[0:128, 0:1]))
            bound_v = s_v.n

            exp_c = {}
            qk_c = {}
            msk_c = {}
            avq = {}
            evq = {}
            nb_glob = [0]
            for Q in range(NQT):
                NBq = 2 * (Q + 1)
                qsl = qsall[:, Q * 512:(Q + 1) * 512]
                for b in range(NBq):
                    n = nb_glob[0]
                    reg = n % 2
                    diag = (b >= NBq - 2)

                    @blk.tensor
                    def _(te, pr=pr, Q=Q, b=b, n=n, reg=reg, prev_ex=prev_ex):
                        if Q == 0 and b == 0:
                            te.wait_ge(m_v, bound_v)
                            te.wait_ge(m_ex, prev_ex)
                        if n >= 2 and (n - 2) in exp_c:
                            te.wait_ge(m_ex, exp_c[n - 2])
                        for j in range(2):
                            kc = 2 * b + j
                            mm = te.matmul(psS[:, reg * 1024 + j * 512:reg * 1024 + (j + 1) * 512],
                                           ks[:, kc * 128:(kc + 1) * 128],
                                           qsl, start=True, stop=True)
                        qk_c[n] = s_qk.inc(mm)

                    @blk.scalar
                    def _(s, pr=pr, Q=Q, b=b, n=n, reg=reg, diag=diag):
                        s.wait_ge(m_qk, qk_c[n])
                        if n >= 2:
                            s.wait_ge(m_av, avq[n - 2])
                        else:
                            s.wait_ge(m_av, prev_av)
                        exp_c[n] = s_ex.inc(
                            s.activation(ps[:, reg * 1024:(reg + 1) * 1024],
                                         psS[:, reg * 1024:(reg + 1) * 1024],
                                         AF.Exp, bias=bcol[:], scale=SCL))

                    if diag:
                        half = b - (NBq - 2)

                        @blk.vector
                        def _(v, n=n, reg=reg, half=half):
                            v.wait_ge(m_ex, exp_c[n])
                            v.wait_ge(m_g, 16)
                            msk_c[n] = s_mk.inc(
                                v.tensor_tensor(ps[:, reg * 1024:(reg + 1) * 1024],
                                                ps[:, reg * 1024:(reg + 1) * 1024],
                                                cmk[:, half * 1024:(half + 1) * 1024],
                                                ALU.mult))

                    @blk.tensor
                    def _(te, pr=pr, Q=Q, b=b, n=n, reg=reg, diag=diag):
                        if diag:
                            te.wait_ge(m_mk, msk_c[n])
                        else:
                            te.wait_ge(m_ex, exp_c[n])
                        if b == 0:
                            te.wait_ge(m_ev, evq.get(Q - 1, 0))
                        for j in range(2):
                            kc = 2 * b + j
                            for qc in range(4):
                                mm = te.matmul(av[qc][:, 0:132],
                                               ps[:, reg * 1024 + j * 512 + qc * 128:
                                                      reg * 1024 + j * 512 + (qc + 1) * 128],
                                               vs[:, kc * VW:(kc + 1) * VW],
                                               start=(b == 0 and j == 0),
                                               stop=(b == NBq - 1 and j == 1))
                        avq[n] = s_av.inc(mm)
                    nb_glob[0] += 1

                @blk.vector
                def _(v, pr=pr, Q=Q):
                    v.wait_ge(m_av, s_av.n)
                    if Q >= 1 or pr >= 1:
                        v.wait_ge(m_oa, s_oa.n)
                    for qc in range(4):
                        v.tensor_copy(ev[:, qc * 132:(qc + 1) * 132], av[qc][:, 0:132])
                    v.drain()
                    for qc in range(4):
                        v.reciprocal(ev[:, qc * 132 + 128:qc * 132 + 129],
                                     ev[:, qc * 132 + 128:qc * 132 + 129])
                    v.drain()
                    for qc in range(4):
                        v.tensor_scalar(ost[:, qc * 128:(qc + 1) * 128],
                                        ev[:, qc * 132:qc * 132 + 128],
                                        ev[:, qc * 132 + 128:qc * 132 + 129],
                                        None, ALU.mult)
                    evq[Q] = s_ev.inc(v.engine_nop())

                @blk.sync
                def _(sy, pr=pr, Q=Q):
                    sy.wait_ge(m_ev, evq[Q])
                    for qc in range(4):
                        sy.dma_start(out=attn_o[4 * Q + qc, :, pr * 128:(pr + 1) * 128],
                                     in_=ost[:, qc * 128:(qc + 1) * 128]).then_inc(m_oa, 16)
                        s_oa.n += 16

        @blk.sync
        def _(sy):
            sy.wait_ge(m_oa, s_oa.n)
            sy.wait_ge(m_o, s_o.n)
            sy.wait_ge(m_osc, s_osc.n)
    return nc



def build_moe0():
    nc = bass.Bass("TRN2", target_bir_lowering=False)
    ap = lambda t: t.ap()
    NHC = HM // 128
    srcp_in = ap(nc.dram_tensor("srcp8", [8, 128, NTC], F32, kind="ExternalInput"))
    xn_all = ap(nc.dram_tensor("xn_all", [T, D], BF16, kind="ExternalInput"))
    w1_in = ap(nc.dram_tensor("w1", [D, HM], F32, kind="ExternalInput"))
    w3_in = ap(nc.dram_tensor("w3", [D, HM], F32, kind="ExternalInput"))
    w2_in = ap(nc.dram_tensor("w2", [HM, D], F32, kind="ExternalInput"))
    idf_in = ap(nc.dram_tensor("identf", [128, 128], F32, kind="ExternalInput"))
    oe_o = ap(nc.dram_tensor("oe", [CAP, D], BF16, kind="ExternalOutput"))

    with ExitStack() as _ctx:
        sp8 = _ctx.enter_context(nc.sbuf_tensor("sp8", [128, 8 * NTC], F32))
        srci = _ctx.enter_context(nc.sbuf_tensor("srci", [128, NTC], F32))
        fci = _ctx.enter_context(nc.sbuf_tensor("fciM", [128, NTC], I32))
        xdt = _ctx.enter_context(nc.sbuf_tensor("xdt", [128, D], BF16))
        xd_fm = _ctx.enter_context(nc.sbuf_tensor("xd_fm", [128, NDC * CAP], BF16))
        hh_fm = _ctx.enter_context(nc.sbuf_tensor("hh_fm", [128, NHC * CAP], BF16))
        w13 = _ctx.enter_context(nc.sbuf_tensor("w13", [128, 2 * 2 * NDC * 128], BF16))
        w2s = _ctx.enter_context(nc.sbuf_tensor("w2s", [128, 2 * 512], BF16))
        s1 = _ctx.enter_context(nc.sbuf_tensor("s1", [128, 512], BF16))
        yst = _ctx.enter_context(nc.sbuf_tensor("yst", [128, 512], BF16))
        idb = _ctx.enter_context(nc.sbuf_tensor("idbM", [128, 128], BF16))
        psH = _ctx.enter_context(nc.psum_tensor("psH", [128, 1024], F32))
        psT = _ctx.enter_context(nc.psum_tensor("psT", [128, 2048], BF16))
        psY = _ctx.enter_context(nc.psum_tensor("psY", [128, 2048], F32))
        m_sm = _ctx.enter_context(nc.semaphore("m_sm"))
        m_w = _ctx.enter_context(nc.semaphore("m_w"))
        m_wa = _ctx.enter_context(nc.semaphore("m_wa"))
        m_wb = _ctx.enter_context(nc.semaphore("m_wb"))
        m_wc = _ctx.enter_context(nc.semaphore("m_wc"))
        m_wd = _ctx.enter_context(nc.semaphore("m_wd"))
        m_g = _ctx.enter_context(nc.semaphore("m_g"))
        m_t = _ctx.enter_context(nc.semaphore("m_t"))
        m_v = _ctx.enter_context(nc.semaphore("m_v"))
        m_s = _ctx.enter_context(nc.semaphore("m_s"))
        m_o = _ctx.enter_context(nc.semaphore("m_o"))
        blk = _BlkW(_ctx.enter_context(nc.Block()))
        s_sm, s_w, s_g = SC(m_sm), SC(m_w), SC(m_g)
        s_t, s_v, s_s, s_o = SC(m_t), SC(m_v), SC(m_s), SC(m_o)
        s_wp = [SC(m_wa), SC(m_wb)]
        m_wp = [m_wa, m_wb]
        s_w2p = [SC(m_wc), SC(m_wd)]
        m_w2p = [m_wc, m_wd]

        @blk.sync
        def _(sy):
            for a in range(8):
                s_sm.inc(sy.dma_start(out=sp8[:, a * NTC:(a + 1) * NTC],
                                      in_=srcp_in[a]), 16)

        @blk.gpsimd
        def _(g):
            s_w.inc(g.dma_start(out=idb[:], in_=idf_in[:, :]), 16)

        @blk.vector
        def _(v):
            v.wait_ge(m_sm, s_sm.n)
            v.tensor_copy(srci[:], sp8[:, 0:NTC]).drain()
            for a in range(1, 8):
                v.tensor_tensor(srci[:], srci[:], sp8[:, a * NTC:(a + 1) * NTC], ALU.min)
                v.drain()
            v.tensor_scalar(srci[:], srci[:], float(T - 1), None, ALU.min).drain()
            s_v.inc(v.tensor_copy(fci[:], srci[:]))
            v.wait_ge(m_v, s_v.n)
            s_v.inc(v.tensor_copy(sp8[:, 0:NTC], fci[:]))

        gath = {}
        tr_ev2 = {}
        hhev = {}
        hh_prev = [None]
        wuse = {}
        w2ld = {}
        w2use = {}
        yevS = {}
        for j in range(NTC):
            @blk.gpsimd
            def _(g, j=j):
                if j == 0:
                    g.wait_ge(m_v, s_v.n)
                else:
                    g.wait_ge(m_t, s_t.n)
                gath[j] = s_g.inc(g.indirect_dma_start(
                    xdt[:, :], None,
                    xn_all[:, :], IndirectOffsetOnAxis(fci[:, j:j + 1], 0)), 16)

            for dc in range(NDC):
                @blk.tensor
                def _(te, j=j, dc=dc):
                    if j == 0 and dc == 0:
                        te.wait_ge(m_w, s_w.n)
                    te.wait_ge(m_g, gath[j])
                    if (j, dc - 2) in tr_ev2:
                        te.wait_ge(m_v, tr_ev2[(j, dc - 2)])
                    elif dc < 2 and (j - 1, NDC - 2 + dc) in tr_ev2:
                        te.wait_ge(m_v, tr_ev2[(j - 1, NDC - 2 + dc)])
                    s_t.inc(te.transpose(psT[:, (dc % 2) * 1024:(dc % 2) * 1024 + 128],
                                         xdt[:, dc * 128:(dc + 1) * 128], idb[:]))

                @blk.vector
                def _(v, j=j, dc=dc):
                    v.wait_ge(m_t, s_t.n)
                    tr_ev2[(j, dc)] = s_v.inc(
                        v.tensor_copy(xd_fm[:, dc * CAP + j * 128:dc * CAP + (j + 1) * 128],
                                      psT[:, (dc % 2) * 1024:(dc % 2) * 1024 + 128]))

        wld = {}
        for hc in range(NHC):
            @blk.gpsimd
            def _(g, hc=hc):
                par = hc % 2
                base = par * 2 * NDC * 128
                if hc >= 2:
                    g.wait_ge(m_t, wuse[hc - 2])
                for q4 in range(4):
                    s_wp[par].inc(g.dma_start(
                        out=w13[:, base + q4 * 4 * 128:base + (q4 + 1) * 4 * 128],
                        in_=w1_in.rearrange("(dc p) h -> p dc h", p=128)[:, q4 * 4:(q4 + 1) * 4,
                                                                        hc * 128:(hc + 1) * 128]), 16)
                for q4 in range(4):
                    s_wp[par].inc(g.dma_start(
                        out=w13[:, base + (16 + q4 * 4) * 128:base + (16 + (q4 + 1) * 4) * 128],
                        in_=w3_in.rearrange("(dc p) h -> p dc h", p=128)[:, q4 * 4:(q4 + 1) * 4,
                                                                        hc * 128:(hc + 1) * 128]), 16)
                wld[hc] = s_wp[par].n

            for tt in range(CAP // 512):
                @blk.tensor
                def _(te, hc=hc, tt=tt):
                    par = hc % 2
                    base = par * 2 * NDC * 128
                    if hc == 0 and tt == 0:
                        te.wait_ge(m_v, s_v.n)
                    te.wait_ge(m_wp[hc % 2], wld[hc])
                    if hh_prev[0] is not None:
                        te.wait_ge(m_v, hh_prev[0])
                    for dc in range(NDC):
                        te.matmul(psH[:, 0:512], w13[:, base + dc * 128:base + (dc + 1) * 128],
                                  xd_fm[:, dc * CAP + tt * 512:dc * CAP + (tt + 1) * 512],
                                  start=(dc == 0), stop=(dc == NDC - 1))
                    for dc in range(NDC):
                        mm = te.matmul(psH[:, 512:1024],
                                       w13[:, base + (16 + dc) * 128:base + (17 + dc) * 128],
                                       xd_fm[:, dc * CAP + tt * 512:dc * CAP + (tt + 1) * 512],
                                       start=(dc == 0), stop=(dc == NDC - 1))
                    s_t.inc(mm)
                    wuse[hc] = s_t.n

                @blk.scalar
                def _(s, hc=hc, tt=tt):
                    s.wait_ge(m_t, s_t.n)
                    if hh_prev[0] is not None:
                        s.wait_ge(m_v, hh_prev[0])
                    s_s.inc(s.activation(s1[:], psH[:, 0:512], AF.Silu))

                @blk.vector
                def _(v, hc=hc, tt=tt):
                    v.wait_ge(m_s, s_s.n)
                    hh_prev[0] = s_v.inc(
                        v.tensor_tensor(hh_fm[:, hc * CAP + tt * 512:hc * CAP + (tt + 1) * 512],
                                        s1[:], psH[:, 512:1024], ALU.mult))

        w2ld = {}
        w2use = {}
        yevS = {}
        for kk in range(16):
            oc, qr = kk // 4, kk % 4
            for hc in range(NHC):
                @blk.gpsimd
                def _(g, oc=oc, qr=qr, hc=hc, kk=kk):
                    par = hc % 2
                    if (kk, hc - 2) in w2use:
                        g.wait_ge(m_t, w2use[(kk, hc - 2)])
                    elif hc < 2 and kk >= 1 and (kk - 1, NHC - 2 + hc) in w2use:
                        g.wait_ge(m_t, w2use[(kk - 1, NHC - 2 + hc)])
                    elif kk == 0 and hc < 2:
                        g.wait_ge(m_t, s_t.n)
                    s_w2p[par].inc(g.dma_start(out=w2s[:, par * 512:(par + 1) * 512],
                                        in_=w2_in[hc * 128:(hc + 1) * 128,
                                                  oc * 512:(oc + 1) * 512]), 16)
                    w2ld[(kk, hc)] = s_w2p[par].n

                @blk.tensor
                def _(te, oc=oc, qr=qr, hc=hc, kk=kk):
                    par = hc % 2
                    te.wait_ge(m_w2p[par], w2ld[(kk, hc)])
                    if hc == 0:
                        te.wait_ge(m_v, s_v.n)
                        if kk >= 1:
                            te.wait_ge(m_s, yevS[kk - 1])
                    for t4 in range(4):
                        tj = qr * 4 + t4
                        mm = te.matmul(psY[:, t4 * 512:(t4 + 1) * 512],
                                       hh_fm[:, hc * CAP + tj * 128:hc * CAP + (tj + 1) * 128],
                                       w2s[:, par * 512:(par + 1) * 512],
                                       start=(hc == 0), stop=(hc == NHC - 1))
                    s_t.inc(mm)
                    w2use[(kk, hc)] = s_t.n

            for t4 in range(4):
                @blk.scalar
                def _(s, oc=oc, qr=qr, kk=kk, t4=t4):
                    if t4 == 0:
                        s.wait_ge(m_t, s_t.n)
                    if kk >= 1 or t4 >= 1:
                        s.wait_ge(m_o, s_o.n)
                    cp = s.copy(yst[:], psY[:, t4 * 512:(t4 + 1) * 512])
                    if t4 == 3:
                        yevS[kk] = s_s.inc(cp)
                    else:
                        s_s.inc(cp)
                    s.wait_ge(m_s, s_s.n)
                    tj = qr * 4 + t4
                    s_o.inc(s.dma_start(
                        out=oe_o.rearrange("(p j) w -> p j w", j=NTC)[:, tj, oc * 512:(oc + 1) * 512],
                        in_=yst[:]), 16)

        @blk.sync
        def _(sy):
            sy.wait_ge(m_o, s_o.n)
    return nc



def build_post0():
    nc = bass.Bass("TRN2", target_bir_lowering=False)
    ap = lambda t: t.ap()
    NHC = HM // 128
    x_in = ap(nc.dram_tensor("x", [TPC, D], F32, kind="ExternalInput"))
    attn_in = ap(nc.dram_tensor("attnm", [TPC, D], BF16, kind="ExternalInput"))
    wo_in = ap(nc.dram_tensor("wo", [D, D], F32, kind="ExternalInput"))
    borow_in = ap(nc.dram_tensor("borow", [1, D], F32, kind="ExternalInput"))
    oe_in = ap(nc.dram_tensor("oe_all", [T, D], BF16, kind="ExternalInput"))
    flatc_in = ap(nc.dram_tensor("flatc", [128, NTC], F32, kind="ExternalInput"))
    gatek_in = ap(nc.dram_tensor("gatek", [128, NTC], F32, kind="ExternalInput"))
    xnfm_in = ap(nc.dram_tensor("xn_fm", [NTC, 128, NDC, 128], BF16, kind="ExternalInput"))
    sw1_in = ap(nc.dram_tensor("sw1", [D, HM], F32, kind="ExternalInput"))
    sw3_in = ap(nc.dram_tensor("sw3", [D, HM], F32, kind="ExternalInput"))
    sw2_in = ap(nc.dram_tensor("sw2", [HM, D], F32, kind="ExternalInput"))
    idf_in = ap(nc.dram_tensor("identf", [128, 128], F32, kind="ExternalInput"))
    x1_o = ap(nc.dram_tensor("x1", [TPC, D], F32, kind="ExternalOutput"))

    with ExitStack() as _ctx:
        xt = _ctx.enter_context(nc.sbuf_tensor("xtP", [128, D], F32))
        at = _ctx.enter_context(nc.sbuf_tensor("atP", [128, D], BF16))
        afm = _ctx.enter_context(nc.sbuf_tensor("afmP", [128, D], BF16))
        mot = _ctx.enter_context(nc.sbuf_tensor("motP", [128, D], BF16))
        x1t = _ctx.enter_context(nc.sbuf_tensor("x1tP", [128, D], F32))

        brow = _ctx.enter_context(nc.sbuf_tensor("browP", [1, D], BF16))
        brtmp = _ctx.enter_context(nc.sbuf_tensor("brtmpP", [1, D], F32))
        gkc = _ctx.enter_context(nc.sbuf_tensor("gkcP", [128, NTC], F32))
        fci = _ctx.enter_context(nc.sbuf_tensor("fciP", [128, NTC], I32))
        flc = _ctx.enter_context(nc.sbuf_tensor("flcP", [128, NTC], F32))
        idb = _ctx.enter_context(nc.sbuf_tensor("idbP", [128, 128], BF16))
        ones_b1 = _ctx.enter_context(nc.sbuf_tensor("onesb1P", [1, 128], BF16))
        xnfm = _ctx.enter_context(nc.sbuf_tensor("xnfmP", [128, 4 * NDC * 128], BF16))
        hh_fm = _ctx.enter_context(nc.sbuf_tensor("hhfmP", [128, NHC * TPC], BF16))
        wos = hh_fm[:, 0:NDC * D]
        w13 = _ctx.enter_context(nc.sbuf_tensor("w13P", [128, 2 * 2 * NDC * 128], BF16))
        w2s = _ctx.enter_context(nc.sbuf_tensor("w2sP", [128, 2 * 512], BF16))
        s1 = _ctx.enter_context(nc.sbuf_tensor("s1P", [128, 512], BF16))
        yst = _ctx.enter_context(nc.sbuf_tensor("ystP", [128, 512], F32))
        psA = _ctx.enter_context(nc.psum_tensor("psAP", [128, 1024], F32))
        psT = _ctx.enter_context(nc.psum_tensor("psTP", [128, 2048], BF16))
        psH = _ctx.enter_context(nc.psum_tensor("psHP", [128, 1024], F32))
        m_sm = _ctx.enter_context(nc.semaphore("m_sm"))
        m_w = _ctx.enter_context(nc.semaphore("m_w"))
        m_x = _ctx.enter_context(nc.semaphore("m_x"))
        m_a = _ctx.enter_context(nc.semaphore("m_a"))
        m_g = _ctx.enter_context(nc.semaphore("m_g"))
        m_t = _ctx.enter_context(nc.semaphore("m_t"))
        m_v = _ctx.enter_context(nc.semaphore("m_v"))
        m_s = _ctx.enter_context(nc.semaphore("m_s"))
        m_o = _ctx.enter_context(nc.semaphore("m_o"))
        m_mo = _ctx.enter_context(nc.semaphore("m_mo"))
        m_wa = _ctx.enter_context(nc.semaphore("m_wa"))
        m_wb = _ctx.enter_context(nc.semaphore("m_wb"))
        m_wc = _ctx.enter_context(nc.semaphore("m_wc"))
        m_wd = _ctx.enter_context(nc.semaphore("m_wd"))
        blk = _BlkW(_ctx.enter_context(nc.Block()))
        s_sm, s_w, s_x, s_a = SC(m_sm), SC(m_w), SC(m_x), SC(m_a)
        s_g, s_t, s_v, s_s = SC(m_g), SC(m_t), SC(m_v), SC(m_s)
        s_o, s_mo = SC(m_o), SC(m_mo)
        s_wp = [SC(m_wa), SC(m_wb)]
        m_wp = [m_wa, m_wb]
        s_w2p = [SC(m_wc), SC(m_wd)]
        m_w2p = [m_wc, m_wd]

        @blk.sync
        def _(sy):
            s_sm.inc(sy.dma_start(out=flc[:], in_=flatc_in[:, :]), 16)
            s_sm.inc(sy.dma_start(out=gkc[:], in_=gatek_in[:, :]), 16)
            s_sm.inc(sy.dma_start(out=brtmp[:], in_=borow_in[:, :]), 16)

        @blk.gpsimd
        def _(g):
            s_w.inc(g.dma_start(out=idb[:], in_=idf_in[:, :]), 16)
            s_w.inc(g.dma_start(out=brow[:], in_=borow_in[:, :]), 16)
            for dc in range(NDC):
                s_w.inc(g.dma_start(out=wos[:, dc * D:(dc + 1) * D],
                                    in_=wo_in[dc * 128:(dc + 1) * 128, :]), 16)

        @blk.vector
        def _(v):
            v.memset(ones_b1[:], 1.0)
            v.wait_ge(m_sm, s_sm.n)
            s_v.inc(v.tensor_copy(fci[:], flc[:]))
            v.wait_ge(m_v, s_v.n)
            s_v.inc(v.tensor_copy(gkc[:, 0:1] if False else flc[:, 0:1], fci[:, 0:1]))

        tr3 = {}
        x1dma = {}
        x1ev = {}
        xn_use = {}
        for tc in range(NTC):
            @blk.sync
            def _(sy, tc=tc):
                if tc >= 1:
                    sy.wait_ge(m_v, tr3.get(('xfree', tc - 1), 0))
                s_x.inc(sy.dma_start(out=xt[:], in_=x_in[tc * 128:(tc + 1) * 128, :]), 16)
                if tc >= 1:
                    sy.wait_ge(m_t, tr3.get(('afree', tc - 1), 0))
                s_a.inc(sy.dma_start(out=at[:], in_=attn_in[tc * 128:(tc + 1) * 128, :]), 16)

            @blk.gpsimd
            def _(g, tc=tc):
                if tc >= 1:
                    g.wait_ge(m_v, tr3.get(('mfree', tc - 1), 0))
                s_g.inc(g.indirect_dma_start(
                    mot[:, :], None, oe_in[:, :],
                    IndirectOffsetOnAxis(fci[:, tc:tc + 1], 0)), 16)

            for dc in range(NDC):
                @blk.tensor
                def _(te, tc=tc, dc=dc):
                    if tc == 0 and dc == 0:
                        te.wait_ge(m_w, s_w.n)
                    te.wait_ge(m_a, s_a.n)
                    if (tc, dc - 2) in tr3:
                        te.wait_ge(m_v, tr3[(tc, dc - 2)])
                    elif dc < 2 and (tc - 1, NDC - 2 + dc) in tr3:
                        te.wait_ge(m_v, tr3[(tc - 1, NDC - 2 + dc)])
                    s_t.inc(te.transpose(psT[:, (dc % 2) * 1024:(dc % 2) * 1024 + 128],
                                         at[:, dc * 128:(dc + 1) * 128], idb[:]))

                @blk.vector
                def _(v, tc=tc, dc=dc):
                    v.wait_ge(m_t, s_t.n)
                    tr3[(tc, dc)] = s_v.inc(
                        v.tensor_copy(afm[:, dc * 128:(dc + 1) * 128],
                                      psT[:, (dc % 2) * 1024:(dc % 2) * 1024 + 128]))
            tr3[('afree', tc)] = s_t.n

            for oc in range(4):
                @blk.tensor
                def _(te, tc=tc, oc=oc):
                    te.wait_ge(m_v, s_v.n)
                    if (tc, oc - 2) in x1ev:
                        te.wait_ge(m_v, x1ev[(tc, oc - 2)])
                    elif tc >= 1 and (tc - 1, oc + 2) in x1ev:
                        te.wait_ge(m_v, x1ev[(tc - 1, oc + 2)])
                    for dc in range(NDC):
                        te.matmul(psA[:, (oc % 2) * 512:(oc % 2) * 512 + 512],
                                  afm[:, dc * 128:(dc + 1) * 128],
                                  wos[:, dc * D + oc * 512:dc * D + (oc + 1) * 512],
                                  start=(dc == 0), stop=False)
                    mm = te.matmul(psA[:, (oc % 2) * 512:(oc % 2) * 512 + 512],
                                   ones_b1[:], brow[0:1, oc * 512:(oc + 1) * 512],
                                   start=False, stop=True)
                    s_t.inc(mm)

                @blk.vector
                def _(v, tc=tc, oc=oc):
                    v.wait_ge(m_t, s_t.n)
                    v.wait_ge(m_x, s_x.n)
                    v.wait_ge(m_g, s_g.n)
                    if tc >= 1 and oc == 0:
                        v.wait_ge(m_o, x1dma[tc - 1])
                    v.tensor_tensor(x1t[:, oc * 512:(oc + 1) * 512],
                                    xt[:, oc * 512:(oc + 1) * 512],
                                    psA[:, (oc % 2) * 512:(oc % 2) * 512 + 512],
                                    ALU.add)
                    v.drain()
                    v.scalar_tensor_tensor(x1t[:, oc * 512:(oc + 1) * 512],
                                           mot[:, oc * 512:(oc + 1) * 512],
                                           gkc[:, tc:tc + 1],
                                           x1t[:, oc * 512:(oc + 1) * 512],
                                           ALU.mult, ALU.add)
                    x1ev[(tc, oc)] = s_v.inc(v.engine_nop())
            tr3[('xfree', tc)] = s_v.n
            tr3[('mfree', tc)] = s_v.n

            @blk.sync
            def _(sy, tc=tc):
                sy.wait_ge(m_v, s_v.n)
                x1dma[tc] = s_o.inc(sy.dma_start(out=x1_o[tc * 128:(tc + 1) * 128, :],
                                                 in_=x1t[:]), 16)

        wld = {}
        wuse = {}
        hh_prev = [None]
        xnld = {}
        for hc in range(NHC):
            @blk.gpsimd
            def _(g, hc=hc):
                par = hc % 2
                base = par * 2 * NDC * 128
                if hc >= 2:
                    g.wait_ge(m_t, wuse[hc - 2])
                for q4 in range(4):
                    s_wp[par].inc(g.dma_start(
                        out=w13[:, base + q4 * 4 * 128:base + (q4 + 1) * 4 * 128],
                        in_=sw1_in.rearrange("(dc p) h -> p dc h", p=128)[:, q4 * 4:(q4 + 1) * 4,
                                                                         hc * 128:(hc + 1) * 128]), 16)
                for q4 in range(4):
                    s_wp[par].inc(g.dma_start(
                        out=w13[:, base + (16 + q4 * 4) * 128:base + (16 + (q4 + 1) * 4) * 128],
                        in_=sw3_in.rearrange("(dc p) h -> p dc h", p=128)[:, q4 * 4:(q4 + 1) * 4,
                                                                         hc * 128:(hc + 1) * 128]), 16)
                wld[hc] = s_wp[par].n

            for tt in range(NTC // 4):
                @blk.sync
                def _(sy, hc=hc, tt=tt):
                    sy.wait_ge(m_t, xn_use.get((hc, tt - 1), xn_use.get((hc - 1, NTC // 4 - 1), 0)))
                    for q in range(4):
                        s_x.inc(sy.dma_start(
                            out=xnfm[:, q * 128 * NDC:(q + 1) * 128 * NDC]
                                .rearrange("p (dc w) -> p dc w", w=128),
                            in_=xnfm_in[tt * 4 + q]), 16)
                    xnld[(hc, tt)] = s_x.n

                @blk.tensor
                def _(te, hc=hc, tt=tt):
                    par = hc % 2
                    base = par * 2 * NDC * 128
                    te.wait_ge(m_wp[par], wld[hc])
                    te.wait_ge(m_x, xnld[(hc, tt)])
                    if hh_prev[0] is not None:
                        te.wait_ge(m_v, hh_prev[0])
                    xv = xnfm.rearrange("p (q c w) -> p q c w", q=4, w=128)
                    for dc in range(NDC):
                        te.matmul(psH[:, 0:512], w13[:, base + dc * 128:base + (dc + 1) * 128],
                                  xv[:, :, dc, :], start=(dc == 0), stop=(dc == NDC - 1))
                    for dc in range(NDC):
                        mm = te.matmul(psH[:, 512:1024],
                                       w13[:, base + (16 + dc) * 128:base + (17 + dc) * 128],
                                       xv[:, :, dc, :], start=(dc == 0), stop=(dc == NDC - 1))
                    s_t.inc(mm)
                    wuse[hc] = s_t.n
                    xn_use[(hc, tt)] = s_t.n

                @blk.scalar
                def _(s, hc=hc, tt=tt):
                    s.wait_ge(m_t, s_t.n)
                    if hh_prev[0] is not None:
                        s.wait_ge(m_v, hh_prev[0])
                    s_s.inc(s.activation(s1[:], psH[:, 0:512], AF.Silu))

                @blk.vector
                def _(v, hc=hc, tt=tt):
                    v.wait_ge(m_s, s_s.n)
                    hh_prev[0] = s_v.inc(
                        v.tensor_tensor(hh_fm[:, hc * TPC + tt * 512:hc * TPC + (tt + 1) * 512],
                                        s1[:], psH[:, 512:1024], ALU.mult))

        w2ld = {}
        w2use = {}
        yevS = {}
        for kk in range(16):
            oc, qr = kk // 4, kk % 4
            for hc in range(NHC):
                @blk.gpsimd
                def _(g, oc=oc, qr=qr, hc=hc, kk=kk):
                    par = hc % 2
                    if (kk, hc - 2) in w2use:
                        g.wait_ge(m_t, w2use[(kk, hc - 2)])
                    elif hc < 2 and kk >= 1 and (kk - 1, NHC - 2 + hc) in w2use:
                        g.wait_ge(m_t, w2use[(kk - 1, NHC - 2 + hc)])
                    elif kk == 0 and hc < 2:
                        g.wait_ge(m_t, s_t.n)
                    s_w2p[par].inc(g.dma_start(out=w2s[:, par * 512:(par + 1) * 512],
                                        in_=sw2_in[hc * 128:(hc + 1) * 128,
                                                   oc * 512:(oc + 1) * 512]), 16)
                    w2ld[(kk, hc)] = s_w2p[par].n

                @blk.tensor
                def _(te, oc=oc, qr=qr, hc=hc, kk=kk):
                    par = hc % 2
                    te.wait_ge(m_w2p[par], w2ld[(kk, hc)])
                    if hc == 0:
                        te.wait_ge(m_v, s_v.n)
                        if kk >= 1:
                            te.wait_ge(m_s, yevS[kk - 1])
                    for t4 in range(4):
                        tj = qr * 4 + t4
                        pbank = psA if t4 < 2 else psH
                        mm = te.matmul(pbank[:, (t4 % 2) * 512:(t4 % 2 + 1) * 512],
                                       hh_fm[:, hc * TPC + tj * 128:hc * TPC + (tj + 1) * 128],
                                       w2s[:, par * 512:(par + 1) * 512],
                                       start=(hc == 0), stop=(hc == NHC - 1))
                    s_t.inc(mm)
                    w2use[(kk, hc)] = s_t.n

            for t4 in range(4):
                @blk.scalar
                def _(s, oc=oc, qr=qr, kk=kk, t4=t4):
                    if t4 == 0:
                        s.wait_ge(m_t, s_t.n)
                    if kk >= 1 or t4 >= 1:
                        s.wait_ge(m_mo, s_mo.n)
                    pbank = psA if t4 < 2 else psH
                    cp = s.copy(yst[:], pbank[:, (t4 % 2) * 512:(t4 % 2 + 1) * 512])
                    if t4 == 3:
                        yevS[kk] = s_s.inc(cp)
                    else:
                        s_s.inc(cp)

                @blk.gpsimd
                def _(g, oc=oc, qr=qr, t4=t4):
                    tj = qr * 4 + t4
                    g.wait_ge(m_s, s_s.n)
                    g.wait_ge(m_o, x1dma[tj])
                    s_mo.inc(g.dma_start(
                        out=x1_o[tj * 128:(tj + 1) * 128, oc * 512:(oc + 1) * 512],
                        in_=yst[:], accum_op=ALU.add), 16)

        @blk.sync
        def _(sy):
            sy.wait_ge(m_o, s_o.n)
            sy.wait_ge(m_mo, s_mo.n)
    return nc



def build_pre1():
    nc = bass.Bass("TRN2", target_bir_lowering=False)
    ap = lambda t: t.ap()
    x_in = ap(nc.dram_tensor("x1", [TPC, D], F32, kind="ExternalInput"))
    wqt_in = ap(nc.dram_tensor("wqt", [D, D], F32, kind="ExternalInput"))
    wkt_in = ap(nc.dram_tensor("wkt", [D, D], F32, kind="ExternalInput"))
    wvt_in = ap(nc.dram_tensor("wvt", [D, D], F32, kind="ExternalInput"))
    cos_in = ap(nc.dram_tensor("cos1", [TPC, D // 2], F32, kind="ExternalInput"))
    sin_in = ap(nc.dram_tensor("sin1", [TPC, D // 2], F32, kind="ExternalInput"))
    idf_in = ap(nc.dram_tensor("identf", [128, 128], F32, kind="ExternalInput"))

    xn_tm_o = ap(nc.dram_tensor("xn1_tm", [TPC, D], BF16, kind="ExternalOutput"))
    xn_fm_o = ap(nc.dram_tensor("xn1_fm", [NTC, 128, NDC, 128], BF16, kind="ExternalOutput"))
    qT_o = ap(nc.dram_tensor("q1T", [NTC, 128, H, 128], BF16, kind="ExternalOutput"))
    kT_o = ap(nc.dram_tensor("k1T", [NTC, 128, H, 128], BF16, kind="ExternalOutput"))
    vch_o = ap(nc.dram_tensor("v1ch", [NTC, 128, H, VW], BF16, kind="ExternalOutput"))
    kf_o = ap(nc.dram_tensor("kf1", [NTC, 128, H, 128], BF16, kind="ExternalOutput"))

    with ExitStack() as _ctx:
        xt = _ctx.enter_context(nc.sbuf_tensor("xt1", [128, D], F32))
        xn32 = _ctx.enter_context(nc.sbuf_tensor("xn321", [128, D], F32))
        xnbf = _ctx.enter_context(nc.sbuf_tensor("xnbf1", [128, D], BF16))
        sv = _ctx.enter_context(nc.sbuf_tensor("sv1", [128, 8], F32))
        cst = _ctx.enter_context(nc.sbuf_tensor("cst1", [128, 1], F32))
        xn_fm = _ctx.enter_context(nc.sbuf_tensor("xnfm1", [128, NDC * TPC], BF16))
        wqs = _ctx.enter_context(nc.sbuf_tensor("wqs1", [128, NDC * D], BF16))
        ct = _ctx.enter_context(nc.sbuf_tensor("ct1", [128, D // 2], F32))
        st = _ctx.enter_context(nc.sbuf_tensor("st1", [128, D // 2], F32))
        idb = _ctx.enter_context(nc.sbuf_tensor("idb1", [128, 128], BF16))
        idf = _ctx.enter_context(nc.sbuf_tensor("idf1", [128, 128], F32))
        qrow = _ctx.enter_context(nc.sbuf_tensor("qrow1", [128, D], F32))
        qrot = _ctx.enter_context(nc.sbuf_tensor("qrot1", [128, D], F32))
        qnb = _ctx.enter_context(nc.sbuf_tensor("qnb1", [128, D], BF16))
        kfb = _ctx.enter_context(nc.sbuf_tensor("kfb1", [128, D], BF16))
        tstg = _ctx.enter_context(nc.sbuf_tensor("tstg1", [128, NDC * 128], BF16))
        vst = _ctx.enter_context(nc.sbuf_tensor("vst1", [128, H * VW], BF16))
        psA = _ctx.enter_context(nc.psum_tensor("psA1", [128, 1024], F32))
        psC = _ctx.enter_context(nc.psum_tensor("psC1", [128, 2048], BF16))
        m_x = _ctx.enter_context(nc.semaphore("m_x"))
        m_rope = _ctx.enter_context(nc.semaphore("m_rope"))
        m_w = _ctx.enter_context(nc.semaphore("m_w"))
        m_sm = _ctx.enter_context(nc.semaphore("m_sm"))
        m_t = _ctx.enter_context(nc.semaphore("m_t"))
        m_v = _ctx.enter_context(nc.semaphore("m_v"))
        m_s = _ctx.enter_context(nc.semaphore("m_s"))
        m_oxn = _ctx.enter_context(nc.semaphore("m_oxn"))
        m_oxf = _ctx.enter_context(nc.semaphore("m_oxf"))
        m_oq = _ctx.enter_context(nc.semaphore("m_oq"))
        m_ok = _ctx.enter_context(nc.semaphore("m_ok"))
        m_ov = _ctx.enter_context(nc.semaphore("m_ov"))
        m_okf = _ctx.enter_context(nc.semaphore("m_okf"))
        blk = _BlkW(_ctx.enter_context(nc.Block()))
        s_x, s_rope, s_w, s_sm = SC(m_x), SC(m_rope), SC(m_w), SC(m_sm)
        s_t, s_v, s_s = SC(m_t), SC(m_v), SC(m_s)
        o_xn, o_xf = SC(m_oxn), SC(m_oxf)
        o_tag = {"q": SC(m_oq), "k": SC(m_ok), "v": SC(m_ov), "kf": SC(m_okf)}
        o_sems = {"q": m_oq, "k": m_ok, "v": m_ov, "kf": m_okf}

        @blk.sync
        def _(sy):
            s_x.inc(sy.dma_start(out=xt[:], in_=x_in[0:128, :]), 16)
            s_sm.inc(sy.dma_start(out=idf[:], in_=idf_in[:, :]), 16)
            s_rope.inc(sy.dma_start(out=ct[:], in_=cos_in[0:128, :]), 16)
            s_rope.inc(sy.dma_start(out=st[:], in_=sin_in[0:128, :]), 16)

        @blk.gpsimd
        def _(g):
            s_w.inc(g.dma_start(out=idb[:], in_=idf_in[:, :]), 16)
            for dc in range(NDC):
                s_w.inc(g.dma_start(out=wqs[:, dc * D:(dc + 1) * D],
                                    in_=wqt_in[dc * 128:(dc + 1) * 128, :]), 16)

        @blk.vector
        def _(v):
            s_v.inc(v.memset(cst[:], EPS))
            for h in range(H):
                o = h * VW
                v.memset(vst[:, o + 129:o + VW], 0.0)
                v.memset(vst[:, o + 128:o + 129], 1.0)

        omil = {}
        evacC = {}
        xtile_free = {}
        for tc in range(NTC):
            @blk.vector
            def _(v, tc=tc):
                v.wait_ge(m_x, 16 * (tc + 1))
                v.reduce_sum(sv[:, 0:1], xt[:], AX.X).drain()
                s_v.inc(v.tensor_scalar_mul(sv[:, 1:2], sv[:, 0:1], 1.0 / D))

            @blk.scalar
            def _(s, tc=tc):
                s.wait_ge(m_x, 16 * (tc + 1))
                if tc >= 1:
                    s.wait_ge(m_v, s_v.n)
                s_s.inc(s.activation(qrot[:], xt[:], AF.Square, accum_out=sv[:, 2:3]))

            @blk.vector
            def _(v, tc=tc):
                v.wait_ge(m_s, s_s.n)
                v.tensor_scalar_mul(sv[:, 3:4], sv[:, 2:3], 1.0 / D).drain()
                v.tensor_scalar(sv[:, 4:5], sv[:, 1:2], sv[:, 1:2], None, ALU.mult).drain()
                s_v.inc(v.tensor_tensor(sv[:, 5:6], sv[:, 3:4], sv[:, 4:5], ALU.subtract))

            @blk.scalar
            def _(s, tc=tc):
                s.wait_ge(m_v, s_v.n)
                s.wait_ge(m_sm, s_sm.n)
                s.activation(sv[:, 6:7], sv[:, 5:6], AF.Ln, bias=cst[:]).drain()
                s_s.inc(s.activation(sv[:, 7:8], sv[:, 6:7], AF.Exp, scale=-0.5))

            @blk.vector
            def _(v, tc=tc):
                v.wait_ge(m_s, s_s.n)
                if ('xntm', tc - 1) in omil:
                    v.wait_ge(m_oxn, omil[('xntm', tc - 1)])
                v.tensor_scalar(xn32[:], xt[:], sv[:, 1:2], sv[:, 7:8],
                                ALU.subtract, ALU.mult).drain()
                s_v.inc(v.tensor_copy(xnbf[:], xn32[:]))
                xtile_free[tc] = s_v.n

            @blk.scalar
            def _(s, tc=tc):
                s.wait_ge(m_v, s_v.n)
                omil[('xntm', tc)] = o_xn.inc(
                    s.dma_start(out=xn_tm_o[tc * 128:(tc + 1) * 128, :], in_=xnbf[:]), 16)

            for dc in range(NDC):
                @blk.tensor
                def _(te, tc=tc, dc=dc):
                    if dc == 0:
                        te.wait_ge(m_v, s_v.n)
                        te.wait_ge(m_w, s_w.n)
                    for key in ((tc, dc - 2), (tc - 1, NDC - 2 + dc) if dc < 2 else None):
                        if key and key in evacC:
                            te.wait_ge(m_v, evacC[key])
                    s_t.inc(te.transpose(psC[:, (dc % 2) * 1024:(dc % 2) * 1024 + 128],
                                         xnbf[:, dc * 128:(dc + 1) * 128], idb[:]))

                @blk.vector
                def _(v, tc=tc, dc=dc):
                    v.wait_ge(m_t, s_t.n)
                    if tc >= 1 and dc == 0:
                        v.wait_ge(m_oxf, omil[('xnfm', tc - 1)])
                    v.tensor_copy(xn_fm[:, dc * TPC + tc * 128: dc * TPC + (tc + 1) * 128],
                                  psC[:, (dc % 2) * 1024:(dc % 2) * 1024 + 128]).drain()
                    evacC[(tc, dc)] = s_v.inc(
                        v.tensor_copy(tstg[:, dc * 128:(dc + 1) * 128],
                                      psC[:, (dc % 2) * 1024:(dc % 2) * 1024 + 128]))

            @blk.sync
            def _(sy, tc=tc):
                sy.wait_ge(m_v, s_v.n)
                omil[('xnfm', tc)] = o_xf.inc(
                    sy.dma_start(out=xn_fm_o[tc],
                                 in_=tstg.rearrange("p (c w) -> p c w", w=128)), 16)
                if tc + 1 < NTC:
                    sy.wait_ge(m_v, xtile_free[tc])
                    s_x.inc(sy.dma_start(out=xt[:],
                                         in_=x_in[(tc + 1) * 128:(tc + 2) * 128, :]), 16)

        state = {"qrow_free": {}, "qnb_free": {}, "rope_ready": {}, "rope_free": {}}

        def proj1(tag, post):
            ev_q = {}
            for tc in range(NTC):
                for oc in range(4):
                    @blk.tensor
                    def _(te, tc=tc, oc=oc, tag=tag):
                        if tc == 0 and oc == 0:
                            te.wait_ge(m_w, s_w.n)
                            te.wait_ge(m_v, s_v.n)
                        k = tc * 4 + oc
                        if k >= 2:
                            te.wait_ge(m_s, ev_q[k - 2])
                        for dc in range(NDC):
                            mm = te.matmul(psA[:, (k % 2) * 512:(k % 2) * 512 + 512],
                                           xn_fm[:, dc * TPC + tc * 128: dc * TPC + (tc + 1) * 128],
                                           wqs[:, dc * D + oc * 512:dc * D + (oc + 1) * 512],
                                           start=(dc == 0), stop=(dc == NDC - 1))
                        s_t.inc(mm)

                    @blk.scalar
                    def _(s, tc=tc, oc=oc, tag=tag):
                        s.wait_ge(m_t, s_t.n)
                        if oc == 0 and (tag, tc - 1) in state["qrow_free"]:
                            s.wait_ge(m_v, state["qrow_free"][(tag, tc - 1)])
                        k = tc * 4 + oc
                        if tag == 'v':
                            if oc == 0 and tc >= 1:
                                s.wait_ge(m_ov, omil[('v', tc - 1)])
                            for j in range(4):
                                h = oc * 4 + j
                                cp = s.copy(vst[:, h * VW:h * VW + 128],
                                            psA[:, (k % 2) * 512 + j * 128:(k % 2) * 512 + (j + 1) * 128])
                            ev_q[k] = s_s.inc(cp)
                        else:
                            ev_q[k] = s_s.inc(s.copy(qrow[:, oc * 512:(oc + 1) * 512],
                                                     psA[:, (k % 2) * 512:(k % 2) * 512 + 512]))
                post(tc)

        def rope1_post(tag, out_dram):
            def post(tc):
                @blk.vector
                def _(v, tc=tc, tag=tag):
                    v.wait_ge(m_s, s_s.n)
                    v.wait_ge(m_rope, state["rope_ready"].get((tag, tc), 32))
                    HALF = D // 2
                    q1 = qrow[:, 0:HALF]
                    q2 = qrow[:, HALF:D]
                    o1 = qrot[:, 0:HALF]
                    o2 = qrot[:, HALF:D]
                    r1 = xn32[:, 0:HALF]
                    r2 = xn32[:, HALF:D]
                    v.tensor_tensor(r1[:, :], q1, ct[:], ALU.mult)
                    v.tensor_tensor(r2[:, :], q2, st[:], ALU.mult).drain()
                    v.tensor_tensor(o1, r1, r2, ALU.subtract).drain()
                    v.tensor_tensor(r1[:, :], q2, ct[:], ALU.mult)
                    v.tensor_tensor(r2[:, :], q1, st[:], ALU.mult).drain()
                    s_v.inc(v.tensor_tensor(o2, r1, r2, ALU.add))
                    state["qrow_free"][(tag, tc)] = s_v.n
                    state["rope_free"][(tag, tc)] = s_v.n
                    v.drain()
                    if (tag, tc - 1) in state["qnb_free"]:
                        v.wait_ge(m_t, state["qnb_free"][(tag, tc - 1)])
                    s_v.inc(v.tensor_copy(qnb[:], qrot[:]))

                @blk.sync
                def _(sy, tc=tc, tag=tag):
                    if tc + 1 < NTC:
                        sy.wait_ge(m_v, state["rope_free"][(tag, tc)])
                        sy.dma_start(out=ct[:], in_=cos_in[(tc + 1) * 128:(tc + 2) * 128, :]).then_inc(m_rope, 16)
                        s_rope.n += 16
                        sy.dma_start(out=st[:], in_=sin_in[(tc + 1) * 128:(tc + 2) * 128, :]).then_inc(m_rope, 16)
                        s_rope.n += 16
                        state["rope_ready"][(tag, tc + 1)] = s_rope.n

                if tag == 'k':
                    @blk.vector
                    def _(v, tc=tc):
                        v.wait_ge(m_v, s_v.n)
                        s_v.inc(v.tensor_scalar(xn32[:], qrot[:], 0.0, None, ALU.min))

                    @blk.scalar
                    def _(s, tc=tc):
                        s.wait_ge(m_v, s_v.n)
                        s_s.inc(s.activation(xnbf[:], xn32[:], AF.Exp))

                    @blk.vector
                    def _(v, tc=tc):
                        v.wait_ge(m_s, s_s.n)
                        if ('kf', tc - 1) in omil:
                            v.wait_ge(m_okf, omil[('kf', tc - 1)])
                        v.tensor_scalar(xn32[:], qrot[:], 0.0, None, ALU.max).drain()
                        s_v.inc(v.tensor_tensor(kfb[:], xn32[:], xnbf[:], ALU.add))

                    @blk.sync
                    def _(sy, tc=tc):
                        sy.wait_ge(m_v, s_v.n)
                        omil[('kf', tc)] = o_tag['kf'].inc(
                            sy.dma_start(out=kf_o[tc],
                                         in_=kfb.rearrange("p (c w) -> p c w", w=128)), 16)

                trev = {}
                for hc in range(H):
                    @blk.tensor
                    def _(te, tc=tc, hc=hc, tag=tag):
                        if hc == 0:
                            te.wait_ge(m_v, s_v.n)
                        if hc - 2 in trev:
                            te.wait_ge(m_s, trev[hc - 2])
                        s_t.inc(te.transpose(psC[:, (hc % 2) * 1024:(hc % 2) * 1024 + 128],
                                             qnb[:, hc * 128:(hc + 1) * 128], idb[:]))

                    @blk.scalar
                    def _(s, tc=tc, hc=hc, tag=tag):
                        s.wait_ge(m_t, s_t.n)
                        if hc == 0 and tc == 0:
                            s.wait_ge(m_oxf, o_xf.n)
                            if tag == 'k':
                                s.wait_ge(m_oq, o_tag['q'].n)
                        if hc == 0 and tc >= 1:
                            s.wait_ge(o_sems[tag], omil[(tag, tc - 1)])
                        trev[hc] = s_s.inc(
                            s.copy(tstg[:, hc * 128:(hc + 1) * 128],
                                   psC[:, (hc % 2) * 1024:(hc % 2) * 1024 + 128]))
                state["qnb_free"][(tag, tc)] = s_t.n

                @blk.sync
                def _(sy, tc=tc, tag=tag):
                    sy.wait_ge(m_s, s_s.n)
                    omil[(tag, tc)] = o_tag[tag].inc(
                        sy.dma_start(out=out_dram[tc],
                                     in_=tstg.rearrange("p (c w) -> p c w", w=128)), 16)
            return post

        def v_post(tc):
            @blk.sync
            def _(sy, tc=tc):
                sy.wait_ge(m_s, s_s.n)
                omil[('v', tc)] = o_tag['v'].inc(
                    sy.dma_start(out=vch_o[tc],
                                 in_=vst.rearrange("p (c w) -> p c w", w=VW)), 16)

        proj1('q', rope1_post('q', qT_o))

        @blk.gpsimd
        def _(g):
            g.wait_ge(m_t, s_t.n)
            for dc in range(NDC):
                s_w.inc(g.dma_start(out=wqs[:, dc * D:(dc + 1) * D],
                                    in_=wkt_in[dc * 128:(dc + 1) * 128, :]), 16)

        @blk.sync
        def _(sy):
            sy.wait_ge(m_v, s_v.n)
            sy.dma_start(out=ct[:], in_=cos_in[0:128, :]).then_inc(m_rope, 16)
            s_rope.n += 16
            sy.dma_start(out=st[:], in_=sin_in[0:128, :]).then_inc(m_rope, 16)
            s_rope.n += 16
        state["rope_ready"][('k', 0)] = s_rope.n
        proj1('k', rope1_post('k', kT_o))

        @blk.gpsimd
        def _(g):
            g.wait_ge(m_t, s_t.n)
            for dc in range(NDC):
                s_w.inc(g.dma_start(out=wqs[:, dc * D:(dc + 1) * D],
                                    in_=wvt_in[dc * 128:(dc + 1) * 128, :]), 16)

        proj1('v', v_post)

        @blk.sync
        def _(sy):
            sy.wait_ge(m_oxn, o_xn.n)
            sy.wait_ge(m_oxf, o_xf.n)
            for t in ("q", "k", "v", "kf"):
                sy.wait_ge(o_sems[t], o_tag[t].n)
    return nc



def build_attn1(seq=SEQ):
    nc = bass.Bass("TRN2", target_bir_lowering=False)
    ap = lambda t: t.ap()
    NP = 4
    KC = seq // 128
    NQT = seq // 512
    SCL = 1.0 / float(np.sqrt(HD))
    qT_in = ap(nc.dram_tensor("q1T", [KC, 128, NP, 128], BF16, kind="ExternalInput"))
    kT_in = ap(nc.dram_tensor("k1T", [KC, 128, NP, 128], BF16, kind="ExternalInput"))
    vch_in = ap(nc.dram_tensor("v1ch", [KC, 128, NP, VW], BF16, kind="ExternalInput"))
    kf_in = ap(nc.dram_tensor("kf1", [KC, 128, NP, 128], BF16, kind="ExternalInput"))
    gater_in = ap(nc.dram_tensor("gater", [128, NP], F32, kind="ExternalInput"))
    comb_o = ap(nc.dram_tensor("comb", [KC, 128, NP * 128], BF16, kind="ExternalOutput"))

    with ExitStack() as _ctx:
        ks = _ctx.enter_context(nc.sbuf_tensor("ksA", [128, KC * 128], BF16))
        vs = _ctx.enter_context(nc.sbuf_tensor("vsA", [128, KC * VW], BF16))
        qsall = _ctx.enter_context(nc.sbuf_tensor("qsallA", [128, KC * 128], BF16))
        kfs = _ctx.enter_context(nc.sbuf_tensor("kfsA", [128, KC * 128], BF16))
        ps = _ctx.enter_context(nc.sbuf_tensor("psA_", [128, 2 * 1024], BF16))
        sq2s = _ctx.enter_context(nc.sbuf_tensor("sq2sA", [128, 512], F32))
        ev = _ctx.enter_context(nc.sbuf_tensor("evA", [128, 4 * 132], F32))
        ost = _ctx.enter_context(nc.sbuf_tensor("ostA", [128, 512], BF16))
        bcol = _ctx.enter_context(nc.sbuf_tensor("bcolA", [128, 1], F32))
        svr = _ctx.enter_context(nc.sbuf_tensor("svrA", [1, 520], F32))
        ones_f = _ctx.enter_context(nc.sbuf_tensor("ones_fA", [128, 1], F32))
        onesb = _ctx.enter_context(nc.sbuf_tensor("onesbA", [1, 128], F32))
        gsig = _ctx.enter_context(nc.sbuf_tensor("gsigA", [128, NP], F32))
        g1sig = _ctx.enter_context(nc.sbuf_tensor("g1sigA", [128, NP], F32))
        memsb = _ctx.enter_context(nc.sbuf_tensor("memsbA", [128, VW], BF16))
        qfb = _ctx.enter_context(nc.sbuf_tensor("qfbA", [128, 512], BF16))
        qfs = _ctx.enter_context(nc.sbuf_tensor("qfsA", [128, 512], F32))
        mos = _ctx.enter_context(nc.sbuf_tensor("mosA", [128, KC * VW], BF16))
        mtmp = _ctx.enter_context(nc.sbuf_tensor("mtmpA", [128, 132], F32))
        psS = _ctx.enter_context(nc.psum_tensor("psSA", [128, 2048], F32))
        av0 = _ctx.enter_context(nc.psum_tensor("av0A", [128, 512], F32))
        av1 = _ctx.enter_context(nc.psum_tensor("av1A", [128, 512], F32))
        av2 = _ctx.enter_context(nc.psum_tensor("av2A", [128, 512], F32))
        av3 = _ctx.enter_context(nc.psum_tensor("av3A", [128, 512], F32))
        m_kv = _ctx.enter_context(nc.semaphore("m_kv"))
        m_sm = _ctx.enter_context(nc.semaphore("m_sm"))
        m_t = _ctx.enter_context(nc.semaphore("m_t"))
        m_v = _ctx.enter_context(nc.semaphore("m_v"))
        m_s = _ctx.enter_context(nc.semaphore("m_s"))
        m_oa = _ctx.enter_context(nc.semaphore("m_oa"))
        m_qk = _ctx.enter_context(nc.semaphore("m_qk"))
        m_ex = _ctx.enter_context(nc.semaphore("m_ex"))
        m_av = _ctx.enter_context(nc.semaphore("m_av"))
        m_ev = _ctx.enter_context(nc.semaphore("m_ev"))
        blk = _BlkW(_ctx.enter_context(nc.Block()))
        s_kv, s_sm, s_t = SC(m_kv), SC(m_sm), SC(m_t)
        s_v, s_s = SC(m_v), SC(m_s)
        s_oa = SC(m_oa)
        s_qk, s_ex, s_av, s_ev = SC(m_qk), SC(m_ex), SC(m_av), SC(m_ev)
        av = [av0, av1, av2, av3]

        @blk.sync
        def _(sy):
            s_sm.inc(sy.dma_start(out=gsig[:], in_=gater_in[:, :]), 16)

        @blk.scalar
        def _(s):
            s.wait_ge(m_sm, s_sm.n)
            s_s.inc(s.activation(gsig[:], gsig[:], AF.Sigmoid))

        @blk.vector
        def _(v):
            v.memset(ones_f[:], 1.0)
            v.memset(onesb[:], 1.0)
            v.wait_ge(m_s, s_s.n)
            v.tensor_scalar(g1sig[:], gsig[:], -1.0, 1.0, ALU.mult, ALU.add)
            s_v.inc(v.engine_nop())

        for pr in range(NP):
            prev_ev = s_ev.n
            prev_ex = s_ex.n
            prev_av = s_av.n
            prev_v = s_v.n
            prev_oa = s_oa.n

            @blk.sync
            def _(sy, pr=pr):
                if pr >= 1:
                    sy.wait_ge(m_qk, s_qk.n)
                    sy.wait_ge(m_av, s_av.n)
                    sy.wait_ge(m_t, s_t.n)
                for kc in range(KC):
                    for dst, src, w in ((ks, kT_in, 128), (vs, vch_in, VW),
                                        (qsall, qT_in, 128), (kfs, kf_in, 128)):
                        sy.dma_start(out=dst[:, kc * w:(kc + 1) * w],
                                     in_=src[kc, :, pr, :]).then_inc(m_kv, 16)
                        s_kv.n += 16

            @blk.tensor
            def _(te, pr=pr):
                te.wait_ge(m_kv, s_kv.n)
                te.wait_ge(m_ev, prev_ev)
                te.wait_ge(m_v, prev_v)
                for kc in range(KC):
                    mm = te.matmul(av0[:, 0:VW], kfs[:, kc * 128:(kc + 1) * 128],
                                   vs[:, kc * VW:(kc + 1) * VW],
                                   start=(kc == 0), stop=(kc == KC - 1))
                s_t.inc(mm)

            @blk.vector
            def _(v, pr=pr):
                v.wait_ge(m_t, s_t.n)
                s_v.inc(v.tensor_copy(memsb[:], av0[:, 0:VW]))

            mo_ev = {}
            for Q in range(NQT):
                @blk.vector
                def _(v, pr=pr, Q=Q):
                    if Q >= 1:
                        v.wait_ge(m_s, s_s.n)
                    s_v.inc(v.tensor_scalar(qfs[:], qsall[:, Q * 512:(Q + 1) * 512],
                                            0.0, None, ALU.min))

                @blk.scalar
                def _(s, pr=pr, Q=Q):
                    s.wait_ge(m_v, s_v.n)
                    s_s.inc(s.activation(sq2s[:], qfs[:], AF.Exp))

                @blk.vector
                def _(v, pr=pr, Q=Q):
                    v.wait_ge(m_s, s_s.n)
                    v.tensor_scalar(qfs[:], qsall[:, Q * 512:(Q + 1) * 512],
                                    0.0, None, ALU.max).drain()
                    s_v.inc(v.tensor_tensor(qfb[:], qfs[:], sq2s[:], ALU.add))

                @blk.tensor
                def _(te, pr=pr, Q=Q):
                    te.wait_ge(m_v, s_v.n)
                    if (pr, Q - 1) in mo_ev:
                        te.wait_ge(m_v, mo_ev[(pr, Q - 1)])
                    b = av1 if Q % 2 == 0 else av2
                    for qc in range(4):
                        col = (qc % 2) * 256
                        bb = b if qc < 2 else (av2 if Q % 2 == 0 else av3)
                        mm = te.matmul(bb[:, col:col + VW],
                                       qfb[:, qc * 128:(qc + 1) * 128],
                                       memsb[:], start=True, stop=True)
                    s_t.inc(mm)

                @blk.vector
                def _(v, pr=pr, Q=Q):
                    v.wait_ge(m_t, s_t.n)
                    cp = None
                    b = av1 if Q % 2 == 0 else av2
                    for qc in range(4):
                        col = (qc % 2) * 256
                        bb = b if qc < 2 else (av2 if Q % 2 == 0 else av3)
                        cp = v.tensor_copy(
                            mos[:, (Q * 4 + qc) * VW:(Q * 4 + qc + 1) * VW],
                            bb[:, col:col + VW])
                    mo_ev[(pr, Q)] = s_v.inc(cp)

            @blk.vector
            def _(v, pr=pr):
                v.wait_ge(m_v, s_v.n)
                v.memset(svr[0:1, 512:513], 0.0)
                s_v.inc(v.memset(svr[0:1, 513:514], 0.0))

            for which, buf in (("k", ks), ("q", qsall)):
                for j in range(NQT):
                    @blk.scalar
                    def _(s, j=j, buf=buf, pr=pr, which=which):
                        s.wait_ge(m_t, s_t.n)
                        s.wait_ge(m_s, s_s.n)
                        s_s.inc(s.activation(sq2s[:], buf[:, j * 512:(j + 1) * 512],
                                             AF.Square))

                    @blk.tensor
                    def _(te, j=j, pr=pr, which=which):
                        te.wait_ge(m_s, s_s.n)
                        te.wait_ge(m_v, s_v.n)
                        s_t.inc(te.matmul(av0[0:1, 0:512], ones_f[:], sq2s[:],
                                          start=True, stop=True))

                    @blk.vector
                    def _(v, j=j, which=which):
                        v.wait_ge(m_t, s_t.n)
                        s_v.inc(v.reduce_max(svr[0:1, 514:515], av0[0:1, 0:512], AX.X))
                        v.wait_ge(m_v, s_v.n)
                        col = 512 if which == "k" else 513
                        s_v.inc(v.tensor_tensor(svr[0:1, col:col + 1], svr[0:1, col:col + 1],
                                                svr[0:1, 514:515], ALU.max))

            @blk.vector
            def _(v, pr=pr):
                v.wait_ge(m_v, s_v.n)
                s_v.inc(v.tensor_tensor(svr[0:1, 515:516], svr[0:1, 512:513],
                                        svr[0:1, 513:514], ALU.mult))

            @blk.scalar
            def _(s, pr=pr):
                s.wait_ge(m_v, s_v.n)
                s_s.inc(s.activation(svr[0:1, 516:517], svr[0:1, 515:516], AF.Sqrt))

            @blk.vector
            def _(v, pr=pr):
                v.wait_ge(m_s, s_s.n)
                s_v.inc(v.tensor_scalar_mul(svr[0:1, 517:518], svr[0:1, 516:517], -SCL))

            @blk.tensor
            def _(te, pr=pr):
                te.wait_ge(m_v, s_v.n)
                s_t.inc(te.matmul(av3[0:128, 0:1], onesb[:, :], svr[0:1, 517:518],
                                  start=True, stop=True))

            @blk.vector
            def _(v, pr=pr):
                v.wait_ge(m_t, s_t.n)
                s_v.inc(v.tensor_copy(bcol[:], av3[0:128, 0:1]))
            bound_v = s_v.n

            exp_c = {}
            qk_c = {}
            avq = {}
            evq = {}
            nglob = [0]
            for Q in range(NQT):
                NB = KC // 2
                qsl = qsall[:, Q * 512:(Q + 1) * 512]
                for b in range(NB):
                    n = nglob[0]
                    reg = n % 2

                    @blk.tensor
                    def _(te, pr=pr, Q=Q, b=b, n=n, reg=reg):
                        if Q == 0 and b == 0:
                            te.wait_ge(m_v, bound_v)
                            te.wait_ge(m_ex, prev_ex)
                        if n >= 2:
                            te.wait_ge(m_ex, exp_c[n - 2])
                        for j in range(2):
                            kc = 2 * b + j
                            mm = te.matmul(psS[:, reg * 1024 + j * 512:reg * 1024 + (j + 1) * 512],
                                           ks[:, kc * 128:(kc + 1) * 128],
                                           qsl, start=True, stop=True)
                        qk_c[n] = s_qk.inc(mm)

                    @blk.scalar
                    def _(s, pr=pr, n=n, reg=reg):
                        s.wait_ge(m_qk, qk_c[n])
                        if n >= 2:
                            s.wait_ge(m_av, avq[n - 2])
                        else:
                            s.wait_ge(m_av, prev_av)
                        exp_c[n] = s_ex.inc(
                            s.activation(ps[:, reg * 1024:(reg + 1) * 1024],
                                         psS[:, reg * 1024:(reg + 1) * 1024],
                                         AF.Exp, bias=bcol[:], scale=SCL))

                    @blk.tensor
                    def _(te, pr=pr, Q=Q, b=b, n=n, reg=reg):
                        te.wait_ge(m_ex, exp_c[n])
                        if b == 0:
                            te.wait_ge(m_ev, evq.get(Q - 1, prev_ev))
                        for j in range(2):
                            kc = 2 * b + j
                            for qc in range(4):
                                mm = te.matmul(av[qc][:, 0:132],
                                               ps[:, reg * 1024 + j * 512 + qc * 128:
                                                      reg * 1024 + j * 512 + (qc + 1) * 128],
                                               vs[:, kc * VW:(kc + 1) * VW],
                                               start=(kc == 0), stop=(kc == KC - 1))
                        avq[n] = s_av.inc(mm)
                    nglob[0] += 1

                @blk.vector
                def _(v, pr=pr, Q=Q):
                    v.wait_ge(m_av, s_av.n)
                    v.wait_ge(m_oa, s_oa.n)
                    for qc in range(4):
                        v.tensor_copy(ev[:, qc * 132:(qc + 1) * 132], av[qc][:, 0:132])
                    v.drain()
                    for qc in range(4):
                        v.reciprocal(ev[:, qc * 132 + 128:qc * 132 + 129],
                                     ev[:, qc * 132 + 128:qc * 132 + 129])
                    v.drain()
                    for qc in range(4):
                        v.tensor_scalar(mtmp[:, 0:128], ev[:, qc * 132:qc * 132 + 128],
                                        ev[:, qc * 132 + 128:qc * 132 + 129],
                                        None, ALU.mult)
                        v.drain()
                        mslice = mos[:, (Q * 4 + qc) * VW:(Q * 4 + qc + 1) * VW]
                        v.reciprocal(mtmp[:, 128:129], mslice[:, 128:129])
                        v.drain()
                        v.tensor_scalar(ost[:, qc * 128:(qc + 1) * 128], mslice[:, 0:128],
                                        mtmp[:, 128:129], gsig[:, pr:pr + 1],
                                        ALU.mult, ALU.mult)
                        v.drain()
                        v.scalar_tensor_tensor(ost[:, qc * 128:(qc + 1) * 128],
                                               mtmp[:, 0:128], g1sig[:, pr:pr + 1],
                                               ost[:, qc * 128:(qc + 1) * 128],
                                               ALU.mult, ALU.add)
                        v.drain()
                    evq[Q] = s_ev.inc(v.engine_nop())

                @blk.sync
                def _(sy, pr=pr, Q=Q):
                    sy.wait_ge(m_ev, evq[Q])
                    for qc in range(4):
                        sy.dma_start(out=comb_o[4 * Q + qc, :, pr * 128:(pr + 1) * 128],
                                     in_=ost[:, qc * 128:(qc + 1) * 128]).then_inc(m_oa, 16)
                        s_oa.n += 16

        @blk.sync
        def _(sy):
            sy.wait_ge(m_oa, s_oa.n)
    return nc



def build_post1():
    nc = bass.Bass("TRN2", target_bir_lowering=False)
    ap = lambda t: t.ap()
    x_in = ap(nc.dram_tensor("x1", [TPC, D], F32, kind="ExternalInput"))
    comb_in = ap(nc.dram_tensor("combm", [TPC, D], BF16, kind="ExternalInput"))
    wot_in = ap(nc.dram_tensor("wo1t", [D, D], F32, kind="ExternalInput"))
    wut_in = ap(nc.dram_tensor("wupt", [D, D], F32, kind="ExternalInput"))
    wdt_in = ap(nc.dram_tensor("wdnt", [D, D], F32, kind="ExternalInput"))
    wex_in = ap(nc.dram_tensor("wexp", [D, D], F32, kind="ExternalInput"))
    xnfm_in = ap(nc.dram_tensor("xn1_fm", [NTC, 128, NDC, 128], BF16, kind="ExternalInput"))
    idf_in = ap(nc.dram_tensor("identf", [128, 128], F32, kind="ExternalInput"))
    x2_o = ap(nc.dram_tensor("x2", [TPC, D], F32, kind="ExternalOutput"))

    with ExitStack() as _ctx:
        xt = _ctx.enter_context(nc.sbuf_tensor("xtQ", [128, D], F32))
        at = _ctx.enter_context(nc.sbuf_tensor("atQ", [128, D], BF16))
        afm = _ctx.enter_context(nc.sbuf_tensor("afmQ", [128, D], BF16))
        x1t = _ctx.enter_context(nc.sbuf_tensor("x1tQ", [128, D], F32))
        wos = _ctx.enter_context(nc.sbuf_tensor("wosQ", [128, NDC * D], BF16))
        w2s = _ctx.enter_context(nc.sbuf_tensor("w2sQ", [128, 32 * 512], BF16))
        xn_fm = _ctx.enter_context(nc.sbuf_tensor("xnfmQ", [128, NDC * TPC], BF16))
        u_fm = wos
        idb = _ctx.enter_context(nc.sbuf_tensor("idbQ", [128, 128], BF16))
        yst = _ctx.enter_context(nc.sbuf_tensor("ystQ", [128, 512], F32))
        psA = _ctx.enter_context(nc.psum_tensor("psAQ", [128, 1024], F32))
        psT = _ctx.enter_context(nc.psum_tensor("psTQ", [128, 2048], BF16))
        psU = _ctx.enter_context(nc.psum_tensor("psUQ", [128, 1024], F32))
        m_w = _ctx.enter_context(nc.semaphore("m_w"))
        m_x = _ctx.enter_context(nc.semaphore("m_x"))
        m_a = _ctx.enter_context(nc.semaphore("m_a"))
        m_t = _ctx.enter_context(nc.semaphore("m_t"))
        m_v = _ctx.enter_context(nc.semaphore("m_v"))
        m_s = _ctx.enter_context(nc.semaphore("m_s"))
        m_o = _ctx.enter_context(nc.semaphore("m_o"))
        m_mo = _ctx.enter_context(nc.semaphore("m_mo"))
        m_wa = _ctx.enter_context(nc.semaphore("m_wa"))
        m_wb = _ctx.enter_context(nc.semaphore("m_wb"))
        blk = _BlkW(_ctx.enter_context(nc.Block()))
        s_w, s_x, s_a = SC(m_w), SC(m_x), SC(m_a)
        s_t, s_v, s_s = SC(m_t), SC(m_v), SC(m_s)
        s_o, s_mo = SC(m_o), SC(m_mo)
        s_wp = [SC(m_wa), SC(m_wb)]
        m_wp = [m_wa, m_wb]

        @blk.gpsimd
        def _(g):
            s_w.inc(g.dma_start(out=idb[:], in_=idf_in[:, :]), 16)
            for dc in range(NDC):
                s_w.inc(g.dma_start(out=wos[:, dc * D:(dc + 1) * D],
                                    in_=wot_in[dc * 128:(dc + 1) * 128, :]), 16)


        @blk.sync
        def _(sy):
            for tc in range(NTC):
                for dc in range(NDC):
                    s_x.inc(sy.dma_start(
                        out=xn_fm[:, dc * TPC + tc * 128:dc * TPC + (tc + 1) * 128],
                        in_=xnfm_in[tc, :, dc, :]), 16)

        tr4 = {}
        x1ev = {}
        x1dma = {}
        for tc in range(NTC):
            @blk.sync
            def _(sy, tc=tc):
                if tc >= 1:
                    sy.wait_ge(m_v, tr4.get(('xfree', tc - 1), 0))
                s_x.inc(sy.dma_start(out=xt[:], in_=x_in[tc * 128:(tc + 1) * 128, :]), 16)
                if tc >= 1:
                    sy.wait_ge(m_t, tr4.get(('afree', tc - 1), 0))
                s_a.inc(sy.dma_start(out=at[:], in_=comb_in[tc * 128:(tc + 1) * 128, :]), 16)

            for dc in range(NDC):
                @blk.tensor
                def _(te, tc=tc, dc=dc):
                    if tc == 0 and dc == 0:
                        te.wait_ge(m_w, s_w.n)
                    te.wait_ge(m_a, s_a.n)
                    if (tc, dc - 2) in tr4:
                        te.wait_ge(m_v, tr4[(tc, dc - 2)])
                    elif dc < 2 and (tc - 1, NDC - 2 + dc) in tr4:
                        te.wait_ge(m_v, tr4[(tc - 1, NDC - 2 + dc)])
                    s_t.inc(te.transpose(psT[:, (dc % 2) * 1024:(dc % 2) * 1024 + 128],
                                         at[:, dc * 128:(dc + 1) * 128], idb[:]))

                @blk.vector
                def _(v, tc=tc, dc=dc):
                    v.wait_ge(m_t, s_t.n)
                    tr4[(tc, dc)] = s_v.inc(
                        v.tensor_copy(afm[:, dc * 128:(dc + 1) * 128],
                                      psT[:, (dc % 2) * 1024:(dc % 2) * 1024 + 128]))
            tr4[('afree', tc)] = s_t.n

            for oc in range(4):
                @blk.tensor
                def _(te, tc=tc, oc=oc):
                    te.wait_ge(m_v, s_v.n)
                    key = (tc, oc - 2) if (tc, oc - 2) in x1ev else ((tc - 1, oc + 2) if (tc - 1, oc + 2) in x1ev else None)
                    if key:
                        te.wait_ge(m_v, x1ev[key])
                    for dc in range(NDC):
                        mm = te.matmul(psA[:, (oc % 2) * 512:(oc % 2) * 512 + 512],
                                       afm[:, dc * 128:(dc + 1) * 128],
                                       wos[:, dc * D + oc * 512:dc * D + (oc + 1) * 512],
                                       start=(dc == 0), stop=(dc == NDC - 1))
                    s_t.inc(mm)

                @blk.vector
                def _(v, tc=tc, oc=oc):
                    v.wait_ge(m_t, s_t.n)
                    v.wait_ge(m_x, s_x.n)
                    if tc >= 1 and oc == 0:
                        v.wait_ge(m_o, x1dma[tc - 1])
                    v.tensor_tensor(x1t[:, oc * 512:(oc + 1) * 512],
                                    xt[:, oc * 512:(oc + 1) * 512],
                                    psA[:, (oc % 2) * 512:(oc % 2) * 512 + 512],
                                    ALU.add)
                    x1ev[(tc, oc)] = s_v.inc(v.engine_nop())
            tr4[('xfree', tc)] = s_v.n

            @blk.sync
            def _(sy, tc=tc):
                sy.wait_ge(m_v, s_v.n)
                x1dma[tc] = s_o.inc(sy.dma_start(out=x2_o[tc * 128:(tc + 1) * 128, :],
                                                 in_=x1t[:]), 16)

        uev = {}
        uld = {}
        uuse = {}
        for uc in range(NDC):
            @blk.gpsimd
            def _(g, uc=uc):
                par = uc % 2
                if uc >= 2:
                    g.wait_ge(m_t, uuse[uc - 2])
                for q4 in range(4):
                    s_wp[par].inc(g.dma_start(
                        out=w2s[:, par * 2048 + q4 * 4 * 128:par * 2048 + (q4 + 1) * 4 * 128],
                        in_=wut_in.rearrange("(dc p) w -> p dc w", p=128)[:, q4 * 4:(q4 + 1) * 4,
                                                                         uc * 128:(uc + 1) * 128]), 16)
                uld[uc] = s_wp[par].n

            for tt in range(4):
                @blk.tensor
                def _(te, uc=uc, tt=tt):
                    te.wait_ge(m_x, s_x.n)
                    te.wait_ge(m_wp[uc % 2], uld[uc])
                    k = uc * 4 + tt
                    if k >= 2:
                        te.wait_ge(m_s, uev[k - 2])
                    par = uc % 2
                    for dc in range(NDC):
                        mm = te.matmul(psU[:, (k % 2) * 512:(k % 2) * 512 + 512],
                                       w2s[:, par * 2048 + dc * 128:par * 2048 + (dc + 1) * 128],
                                       xn_fm[:, dc * TPC + tt * 512:dc * TPC + (tt + 1) * 512],
                                       start=(dc == 0), stop=(dc == NDC - 1))
                    s_t.inc(mm)
                    uuse[uc] = s_t.n

                @blk.scalar
                def _(s, uc=uc, tt=tt):
                    s.wait_ge(m_t, s_t.n)
                    k = uc * 4 + tt
                    uev[k] = s_s.inc(s.activation(
                        u_fm[:, uc * TPC + tt * 512:uc * TPC + (tt + 1) * 512],
                        psU[:, (k % 2) * 512:(k % 2) * 512 + 512], AF.Relu))

        yevS = {}
        wyld = {}
        qr_first = {oc: oc * 4 for oc in range(4)}
        for kk in range(16):
            oc, qr = kk // 4, kk % 4

            @blk.gpsimd
            def _(g, oc=oc, kk=kk):
                if qr_first[oc] == kk:
                    g.wait_ge(m_t, s_t.n)
                    for q4 in range(4):
                        s_w.inc(g.dma_start(
                            out=w2s[:, q4 * 4 * 512:(q4 + 1) * 4 * 512],
                            in_=wdt_in.rearrange("(dc p) w -> p dc w", p=128)[:, q4 * 4:(q4 + 1) * 4,
                                                                             oc * 512:(oc + 1) * 512]), 16)
                    for q4 in range(4):
                        s_w.inc(g.dma_start(
                            out=w2s[:, (16 + q4 * 4) * 512:(16 + (q4 + 1) * 4) * 512],
                            in_=wex_in.rearrange("(dc p) w -> p dc w", p=128)[:, q4 * 4:(q4 + 1) * 4,
                                                                             oc * 512:(oc + 1) * 512]), 16)
                wyld[kk] = s_w.n

            @blk.tensor
            def _(te, oc=oc, qr=qr, kk=kk):
                te.wait_ge(m_w, wyld[kk])
                te.wait_ge(m_s, s_s.n)
                if kk >= 1:
                    te.wait_ge(m_s, yevS[kk - 1])
                for t4 in range(4):
                    tj = qr * 4 + t4
                    pbank = psA if t4 < 2 else psU
                    for dc in range(NDC):
                        te.matmul(pbank[:, (t4 % 2) * 512:(t4 % 2 + 1) * 512],
                                  u_fm[:, dc * TPC + tj * 128:dc * TPC + (tj + 1) * 128],
                                  w2s[:, dc * 512:(dc + 1) * 512],
                                  start=(dc == 0), stop=False)
                    for dc in range(NDC):
                        mm = te.matmul(pbank[:, (t4 % 2) * 512:(t4 % 2 + 1) * 512],
                                       xn_fm[:, dc * TPC + tj * 128:dc * TPC + (tj + 1) * 128],
                                       w2s[:, (16 + dc) * 512:(17 + dc) * 512],
                                       start=False, stop=(dc == NDC - 1))
                s_t.inc(mm)

            for t4 in range(4):
                @blk.scalar
                def _(s, oc=oc, qr=qr, kk=kk, t4=t4):
                    if t4 == 0:
                        s.wait_ge(m_t, s_t.n)
                    if kk >= 1 or t4 >= 1:
                        s.wait_ge(m_mo, s_mo.n)
                    pbank = psA if t4 < 2 else psU
                    cp = s.copy(yst[:], pbank[:, (t4 % 2) * 512:(t4 % 2 + 1) * 512])
                    if t4 == 3:
                        yevS[kk] = s_s.inc(cp)
                    else:
                        s_s.inc(cp)

                @blk.gpsimd
                def _(g, oc=oc, qr=qr, t4=t4):
                    tj = qr * 4 + t4
                    g.wait_ge(m_s, s_s.n)
                    g.wait_ge(m_o, x1dma[tj])
                    s_mo.inc(g.dma_start(
                        out=x2_o[tj * 128:(tj + 1) * 128, oc * 512:(oc + 1) * 512],
                        in_=yst[:], accum_op=ALU.add), 16)

        @blk.sync
        def _(sy):
            sy.wait_ge(m_o, s_o.n)
            sy.wait_ge(m_mo, s_mo.n)
    return nc


def kernel(**inputs):
    full = inputs
    nc1 = build_pre0()
    maps1 = pre0_inputs(full)
    r1 = run_bass_kernel_spmd(nc1, maps1, core_ids=list(range(NCORE))).results

    ident = np.eye(128, dtype=np.float32)
    cm = np.zeros((128, 2048), np.float32)
    for half in range(2):
        for i2 in range(2):
            off = 128 * (2 * half + i2)
            qf = np.arange(512)[None, :]
            kp = np.arange(128)[:, None]
            cm[:, half * 1024 + i2 * 512: half * 1024 + (i2 + 1) * 512] = (qf >= kp + off)
    cnts = np.stack([np.asarray(r1[c]["cnt"]).ravel() for c in range(NCORE)]).astype(np.float32)
    maps2 = []
    for c in range(NCORE):
        g0 = (c // 4) * 4
        hsl = slice((c % 4) * 4, (c % 4) * 4 + 4)
        qnT = np.concatenate([np.asarray(r1[g0 + i]["qnT"]).reshape(NTC, 128, H, 128)
                              for i in range(4)], 0)[:, :, hsl, :]
        knT = np.concatenate([np.asarray(r1[g0 + i]["knT"]).reshape(NTC, 128, H, 128)
                              for i in range(4)], 0)[:, :, hsl, :]
        vch = np.concatenate([np.asarray(r1[g0 + i]["vch"]).reshape(NTC, 128, H, VW)
                              for i in range(4)], 0)[:, :, hsl, :]
        maps2.append({
            "qnT": np.ascontiguousarray(qnT), "knT": np.ascontiguousarray(knT),
            "vch": np.ascontiguousarray(vch), "cnts": cnts,
            "cmask": cm,
            "eidx": np.asarray(r1[c]["eidx"]), "pos": np.asarray(r1[c]["pos"]),
            "gate": np.asarray(r1[c]["gate"]),
            "tokid": np.ascontiguousarray(
                np.arange(c * TPC, (c + 1) * TPC, dtype=np.float32).reshape(NTC, 128).T),
            "pmask": (np.arange(8) < c).astype(np.float32)[:, None],
            "identf": ident,
        })
    nc2 = build_attn0()
    r2 = run_bass_kernel_spmd(nc2, maps2, core_ids=list(range(NCORE))).results

    xn_all = np.concatenate([np.asarray(r1[c]["xn_tm"]) for c in range(NCORE)], 0)
    srcp = np.stack([np.asarray(r2[c]["srcp"]).ravel() for c in range(NCORE)])
    maps3 = []
    for c in range(NCORE):
        slab = srcp[:, c * CAP:(c + 1) * CAP]
        maps3.append({
            "srcp8": np.ascontiguousarray(slab.reshape(8, 128, NTC)),
            "xn_all": xn_all,
            "w1": np.asarray(full["L0_ffn_W1"], np.float32)[c],
            "w3": np.asarray(full["L0_ffn_W3"], np.float32)[c],
            "w2": np.asarray(full["L0_ffn_W2"], np.float32)[c],
            "identf": ident,
        })
    nc3 = build_moe0()
    r3 = run_bass_kernel_spmd(nc3, maps3, core_ids=list(range(NCORE))).results

    oe_all = np.concatenate([np.asarray(r3[c]["oe"]) for c in range(NCORE)], 0)
    x_flat = np.asarray(full["L0_attn_x"], np.float32).reshape(T, D)
    attn_full = {}
    for g in range(2):
        parts = [np.asarray(r2[4 * g + i]["attn"]).reshape(SEQ // 128, 128, 512)
                 for i in range(4)]
        attn_full[g] = np.concatenate(parts, 2).reshape(SEQ, D)
    maps4 = []
    for c in range(NCORE):
        g = c // 4
        rows = attn_full[g][(c % 4) * TPC:(c % 4 + 1) * TPC]
        maps4.append({
            "x": np.ascontiguousarray(x_flat[c * TPC:(c + 1) * TPC]),
            "attnm": np.ascontiguousarray(rows.astype(BF)),
            "wo": np.asarray(full["L0_attn_Wo"], np.float32),
            "borow": np.asarray(full["L0_attn_bo"], np.float32)[None, :],
            "oe_all": oe_all,
            "flatc": np.asarray(r2[c]["flatc"]),
            "gatek": np.asarray(r2[c]["gatek"]),
            "xn_fm": np.asarray(r1[c]["xn_fm"]).reshape(NTC, 128, NDC, 128),
            "sw1": np.asarray(full["L0_ffn_sw1"], np.float32),
            "sw3": np.asarray(full["L0_ffn_sw3"], np.float32),
            "sw2": np.asarray(full["L0_ffn_sw2"], np.float32),
            "identf": ident,
        })
    nc4 = build_post0()
    r4 = run_bass_kernel_spmd(nc4, maps4, core_ids=list(range(NCORE))).results

    pos64 = np.arange(SEQ, dtype=np.float64)
    inv = 1.0 / (10000.0 ** (np.arange(0, D, 2, dtype=np.float64) / D))
    ang1 = pos64[:, None] * inv[None, :]
    cos1 = np.cos(ang1).astype(np.float32)
    sin1 = np.sin(ang1).astype(np.float32)
    wqt = np.ascontiguousarray(np.asarray(full["L1_attn_Wq"], np.float32).T)
    wkt = np.ascontiguousarray(np.asarray(full["L1_attn_Wk"], np.float32).T)
    wvt = np.ascontiguousarray(np.asarray(full["L1_attn_Wv"], np.float32).T)
    maps5 = []
    for c in range(NCORE):
        posc = (np.arange(c * TPC, (c + 1) * TPC) % SEQ)
        maps5.append({
            "x1": np.asarray(r4[c]["x1"]),
            "wqt": wqt, "wkt": wkt, "wvt": wvt,
            "cos1": np.ascontiguousarray(cos1[posc]),
            "sin1": np.ascontiguousarray(sin1[posc]),
            "identf": ident,
        })
    nc5 = build_pre1()
    r5 = run_bass_kernel_spmd(nc5, maps5, core_ids=list(range(NCORE))).results

    gate_raw = np.asarray(full["L1_attn_gate"], np.float32).reshape(H)
    maps6 = []
    for c in range(NCORE):
        g0 = (c // 4) * 4
        hsl = slice((c % 4) * 4, (c % 4) * 4 + 4)
        def asm(key, w):
            return np.ascontiguousarray(np.concatenate(
                [np.asarray(r5[g0 + i][key]).reshape(NTC, 128, H, w) for i in range(4)],
                0)[:, :, hsl, :])
        maps6.append({
            "q1T": asm("q1T", 128), "k1T": asm("k1T", 128),
            "v1ch": asm("v1ch", VW), "kf1": asm("kf1", 128),
            "gater": np.broadcast_to(gate_raw[hsl][None, :], (128, 4)).copy(),
        })
    nc6 = build_attn1()
    r6 = run_bass_kernel_spmd(nc6, maps6, core_ids=list(range(NCORE))).results

    comb_full = {}
    for g in range(2):
        parts = [np.asarray(r6[4 * g + i]["comb"]).reshape(SEQ // 128, 128, 512)
                 for i in range(4)]
        comb_full[g] = np.concatenate(parts, 2).reshape(SEQ, D)
    wot = np.ascontiguousarray(np.asarray(full["L1_attn_Wo"], np.float32).T)
    wut = np.ascontiguousarray(np.asarray(full["L1_ffn_W_up"], np.float32).T)
    wdt = np.ascontiguousarray(np.asarray(full["L1_ffn_W_down"], np.float32).T)
    wex = np.asarray(full["L1_ffn_W_expert"], np.float32)
    maps7 = []
    for c in range(NCORE):
        g = c // 4
        rows = comb_full[g][(c % 4) * TPC:(c % 4 + 1) * TPC]
        maps7.append({
            "x1": np.asarray(r4[c]["x1"]),
            "combm": np.ascontiguousarray(rows.astype(BF)),
            "wo1t": wot, "wupt": wut, "wdnt": wdt, "wexp": wex,
            "xn1_fm": np.asarray(r5[c]["xn1_fm"]).reshape(NTC, 128, NDC, 128),
            "identf": ident,
        })
    nc7 = build_post1()
    r7 = run_bass_kernel_spmd(nc7, maps7, core_ids=list(range(NCORE))).results

    out = np.concatenate([np.asarray(r7[c]["x2"]) for c in range(NCORE)], 0)
    return out.reshape(2, SEQ, D).astype(np.float32)
